# version 10
# baseline (speedup 1.0000x reference)
import numpy as np
from contextlib import ExitStack

import concourse.bass as bass
import concourse.mybir as mybir
from concourse.bass_utils import run_bass_kernel_spmd

F32 = mybir.dt.float32
BF16 = mybir.dt.bfloat16
AF = mybir.ActivationFunctionType
ALU = mybir.AluOpType
AX = mybir.AxisListType

D = 1024
H = 16
SEQ = 4096
TOK = 2048
NPAIR = 8
DC = 8
NCH = 8
QCH = 4
E_MOE = 8
FF = 1408
FC = 11
NEXP = 10
SHIFT = 16.0
CUT = 100.0
SLOPES = [0.5 ** h for h in range(H)]
LN_EPS = 1e-5
NSLOT = 8
ENGINES = ("sync", "gpsimd", "vector", "scalar", "tensor")


def _alive(h, kt, qc):
    ktr = kt % 16
    dlo = qc * 512 - ktr * 128 - 127
    dhi = qc * 512 + 511 - ktr * 128
    if kt < 16:
        dmin = 0 if (dlo <= 0 <= dhi) else min(abs(dlo), abs(dhi))
    else:
        dmin = 2048 - max(-dlo, dhi)
    return SLOPES[h] * dmin <= CUT


class Prog:
    def __init__(self):
        self.ops = {e: [] for e in ENGINES}
        self.cnt = {}

    def add(self, eng, fn, waits=(), inc=None, inc_by=1):
        ws = tuple((s, v) for sv_ in waits if sv_ is not None for (s, v) in [sv_]
                   if (s is not None) and v > 0)
        self.ops[eng].append((fn, ws, inc, inc_by))
        if inc is not None:
            self.cnt[inc] = self.cnt.get(inc, 0) + inc_by
            return (inc, self.cnt[inc])
        return None

    def fence(self, sems=("tv", "vv", "sv", "gv")):
        return [(s, self.cnt.get(s, 0)) for s in sems]

    def emit(self, block, semh):
        for ename in ENGINES:
            ops = self.ops[ename]

            def body(eng, _ops=ops, _ename=ename):
                sep = _ename in ("vector", "scalar")
                hw = {}
                for fn, waits, inc, inc_by in _ops:
                    if sep:
                        for _ in range(6):
                            eng.nop()
                    for (s, v) in waits:
                        if hw.get(s, 0) < v:
                            eng.wait_ge(semh[s], v)
                            hw[s] = v
                    instr = fn(eng)
                    if inc is not None:
                        instr.then_inc(semh[inc], inc_by)

            getattr(block, ename)(body)


def build(debug=False):
    nc = bass.Bass("TRN2", target_bir_lowering=False,
                   detect_race_conditions=False)

    x_d = nc.dram_tensor("x", [SEQ, D], F32, kind="ExternalInput").ap()
    svec_d = nc.dram_tensor("svec", [128, 1], F32, kind="ExternalInput").ap()
    qkvwt_d = nc.dram_tensor("qkvwt", [D, 3072], F32, kind="ExternalInput").ap()
    qkvb_d = nc.dram_tensor("qkvb", [128, 24], F32, kind="ExternalInput").ap()
    outwt_d = nc.dram_tensor("outwt", [D, D], F32, kind="ExternalInput").ap()
    outb_d = nc.dram_tensor("outb", [128, D], F32, kind="ExternalInput").ap()
    gwt_d = nc.dram_tensor("gwt", [D, 8], F32, kind="ExternalInput").ap()
    wg_d = nc.dram_tensor("wg", [E_MOE, D, FF], F32, kind="ExternalInput").ap()
    wu_d = nc.dram_tensor("wu", [E_MOE, D, FF], F32, kind="ExternalInput").ap()
    wd_d = nc.dram_tensor("wd", [E_MOE, FF, D], F32, kind="ExternalInput").ap()
    swg_d = nc.dram_tensor("swg", [2, D, FF], F32, kind="ExternalInput").ap()
    swu_d = nc.dram_tensor("swu", [2, D, FF], F32, kind="ExternalInput").ap()
    swd_d = nc.dram_tensor("swd", [2, FF, D], F32, kind="ExternalInput").ap()
    out_d = nc.dram_tensor("out", [TOK, D], F32, kind="ExternalOutput").ap()
    dbg = {}
    if debug:
        for name, shp in [("dbg_xn", [128, 2048]), ("dbg_logits", [128, 128]),
                          ("dbg_comb", [128, 128]), ("dbg_kp", [128, 4096]),
                          ("dbg_attn", [128, 2048]), ("dbg_y", [128, 1024]),
                          ("dbg_q", [128, 2048]), ("dbg_vp", [128, 4160]),
                          ("dbg_ptf", [128, 16384]), ("dbg_sh", [2, 2048]),
                          ("dbg_tt", [128, 4096]), ("dbg_pb", [128, 4096]),
                          ("dbg_mx", [128, 8]),
                          ("dbg_attnf", [128, 16384]),
                          ("dbg_x0", [128, 1024]), ("dbg_xa", [128, 1024]),
                          ("dbg_st", [128, 192])]:
            dbg[name] = nc.dram_tensor(name, shp, F32, kind="ExternalOutput").ap()

    P = Prog()
    SEMNAMES = ["dxa", "dxe", "dws", "dwd", "dsm", "dout", "dbgo",
                "tv", "vv", "sv", "gv"]

    with ExitStack() as es:
        semh = {s: es.enter_context(nc.semaphore(s)) for s in SEMNAMES}

        ident = es.enter_context(nc.sbuf_tensor("ident", [128, 128], BF16))
        svec = es.enter_context(nc.sbuf_tensor("svec_s", [128, 1], F32))
        qkvb = es.enter_context(nc.sbuf_tensor("qkvb_s", [128, 24], F32))
        gw32 = es.enter_context(nc.sbuf_tensor("gw32", [128, 64], F32))
        gwhi = es.enter_context(nc.sbuf_tensor("gwhi", [128, 64], BF16))
        gwlo = es.enter_context(nc.sbuf_tensor("gwlo", [128, 64], BF16))
        comb = es.enter_context(nc.sbuf_tensor("comb", [128, 128], F32))
        xn_own = es.enter_context(nc.sbuf_tensor("xn_own", [128, DC * TOK], BF16))
        attno = es.enter_context(nc.sbuf_tensor("attno", [128, DC * TOK], BF16))
        cst = es.enter_context(nc.sbuf_tensor("cst", [128, 24], F32))

        constvals = {}
        cst_tok = [None]

        def cbias(v):
            v = float(v)
            if v == 0.0:
                return v
            if v not in constvals:
                i = len(constvals)
                assert i < 24
                constvals[v] = i
                cst_tok[0] = P.add("gpsimd", (lambda e, _i=i, _v=v: e.memset(
                    cst[:, _i:_i + 1], _v)), inc="gv")
                nc.const_aps.register(F32, v, cst[:, i:i + 1])
            return v

        esB = ExitStack()
        xn_oth = esB.enter_context(nc.sbuf_tensor("xn_oth", [128, DC * TOK], BF16))

        tk_im = P.add("gpsimd", lambda e: e.memset(ident[:, :], 0.0),
                      inc="gv")
        tk_ident = P.add("gpsimd", lambda e: e.affine_select(
            out=ident[:, :], in_=ident[:, :], compare_op=ALU.not_equal,
            fill=1.0, base=0, pattern=[[-1, 128]], channel_multiplier=1),
            waits=[tk_im], inc="gv")

        P.add("sync", lambda e: e.dma_start(out=svec[:, :], in_=svec_d),
              inc="dsm", inc_by=16)
        P.add("sync", lambda e: e.dma_start(out=qkvb[:, :], in_=qkvb_d),
              inc="dsm", inc_by=16)
        P.add("sync", lambda e: e.dma_start(
            out=gw32.ap().rearrange("p (dc e) -> p dc e", dc=DC),
            in_=gwt_d.rearrange("(dc p) e -> p dc e", p=128)),
            inc="dsm", inc_by=16)
        nsm = 3

        tk_gwhi = P.add("scalar", lambda e: e.activation(
            out=gwhi[:, :], in_=gw32[:, :], func=AF.Identity, bias=0.0,
            scale=1.0), waits=[("dsm", 48)], inc="sv")
        tk_gwlo = P.add("vector", lambda e: e.tensor_tensor(
            out=gwlo[:, :], in0=gw32[:, :], in1=gwhi[:, :], op=ALU.subtract),
            waits=[tk_gwhi], inc="vv")

        esA = ExitStack()
        esG = es if debug else esA
        stats = esG.enter_context(nc.sbuf_tensor("stats", [128, 192], F32))
        logits = esG.enter_context(nc.sbuf_tensor("logits", [128, 128], F32))
        gb = esG.enter_context(nc.sbuf_tensor("gb", [128, 96], F32))
        pmb = esG.enter_context(nc.sbuf_tensor("pmb", [128, 128], F32))
        scr1 = esA.enter_context(nc.sbuf_tensor("scr1", [128, 1024], F32))
        xt = esA.enter_context(nc.sbuf_tensor("xt", [128, 2 * D], F32))
        xn32 = esA.enter_context(nc.sbuf_tensor("xn32", [128, 2 * D], F32))
        hib = esA.enter_context(nc.sbuf_tensor("hib", [128, 2 * D], BF16))
        lob = esA.enter_context(nc.sbuf_tensor("lob", [128, D], BF16))
        lofm = esA.enter_context(nc.sbuf_tensor("lofm", [128, D], BF16))
        psa = [esA.enter_context(nc.psum_tensor("psa0", [128, 512], BF16)),
               esA.enter_context(nc.psum_tensor("psa1", [128, 512], BF16))]
        psr = esA.enter_context(nc.psum_tensor("psr", [128, 512], F32))

        SUMS, SQ, MU, VARP, RSTD, TMP = 0, 32, 64, 96, 128, 160
        col = lambda base, t: stats[:, base + t: base + t + 1]

        xt_vg = [None, None]
        xt_sg = [None, None]
        psa_g = [None, None]
        psr_g = None
        hib_tg = [None, None]
        hib_vg = [None, None]
        xn32_sg = [None, None]
        lob_tg = None
        lofm_tg = None
        nxa = 0
        tk_router = []

        for t in range(32):
            sl = t % 2
            xsl = xt[:, sl * D:(sl + 1) * D]
            w = [x for x in (xt_vg[sl], xt_sg[sl]) if x]
            P.add("sync", (lambda e, _t=t, _x=xsl: e.dma_start(
                out=_x, in_=x_d[_t * 128:(_t + 1) * 128, :])), waits=w,
                inc="dxa", inc_by=16)
            nxa += 1
            tk_x = ("dxa", 16 * nxa)

            tk_sum = P.add("vector", (lambda e, _t=t, _x=xsl: e.reduce_sum(
                col(SUMS, _t), _x, axis=AX.X)), waits=[tk_x], inc="vv")
            tk_sq = P.add("scalar", (lambda e, _t=t, _x=xsl: e.activation(
                out=scr1[:, :], in_=_x, func=AF.Square, bias=0.0, scale=1.0,
                accum_out=col(SQ, _t))), waits=[tk_x], inc="sv")
            xt_sg[sl] = tk_sq
            P.add("vector", (lambda e, _t=t: e.tensor_scalar(
                out=col(MU, _t), in0=col(SUMS, _t), scalar1=1.0 / D,
                scalar2=None, op0=ALU.mult)), waits=[tk_sum], inc="vv")
            P.add("vector", (lambda e, _t=t: e.tensor_tensor(
                out=col(TMP, _t), in0=col(MU, _t), in1=col(MU, _t),
                op=ALU.mult)), inc="vv")
            P.add("vector", (lambda e, _t=t: e.tensor_scalar(
                out=col(VARP, _t), in0=col(SQ, _t), scalar1=1.0 / D,
                scalar2=LN_EPS, op0=ALU.mult, op1=ALU.add)),
                waits=[tk_sq], inc="vv")
            tk_v = P.add("vector", (lambda e, _t=t: e.tensor_tensor(
                out=col(VARP, _t), in0=col(VARP, _t), in1=col(TMP, _t),
                op=ALU.subtract)), inc="vv")
            tk_sd = P.add("scalar", (lambda e, _t=t: e.activation(
                out=col(TMP, _t), in_=col(VARP, _t), func=AF.Sqrt, bias=0.0,
                scale=1.0)), waits=[tk_v], inc="sv")
            P.add("vector", (lambda e, _t=t: e.reciprocal(
                col(RSTD, _t), col(TMP, _t))), waits=[tk_sd], inc="vv")
            P.add("vector", (lambda e, _t=t: e.tensor_tensor(
                out=col(TMP, _t), in0=col(RSTD, _t), in1=col(RSTD, _t),
                op=ALU.mult)), inc="vv")
            P.add("vector", (lambda e, _t=t: e.tensor_tensor(
                out=col(TMP, _t), in0=col(TMP, _t), in1=col(VARP, _t),
                op=ALU.mult)), inc="vv")
            P.add("vector", (lambda e, _t=t: e.tensor_scalar(
                out=col(TMP, _t), in0=col(TMP, _t), scalar1=-0.5, scalar2=1.5,
                op0=ALU.mult, op1=ALU.add)), inc="vv")
            tk_rstd = P.add("vector", (lambda e, _t=t: e.tensor_tensor(
                out=col(RSTD, _t), in0=col(RSTD, _t), in1=col(TMP, _t),
                op=ALU.mult)), inc="vv")
            w = [tk_rstd]
            if xn32_sg[sl]:
                w.append(xn32_sg[sl])
            tk_xn = P.add("vector", (lambda e, _t=t, _x=xsl, _s=sl:
                e.tensor_scalar(
                    out=xn32[:, _s * D:(_s + 1) * D], in0=_x,
                    scalar1=col(MU, _t), scalar2=col(RSTD, _t),
                    op0=ALU.subtract, op1=ALU.mult)),
                waits=w, inc="vv")
            xt_vg[sl] = tk_xn
            w = [tk_xn]
            if hib_tg[sl]:
                w.append(hib_tg[sl])
            if hib_vg[sl]:
                w.append(hib_vg[sl])
            tk_hi = P.add("scalar", (lambda e, _s=sl: e.activation(
                out=hib[:, _s * D:(_s + 1) * D],
                in_=xn32[:, _s * D:(_s + 1) * D], func=AF.Identity, bias=0.0,
                scale=1.0)), waits=w, inc="sv")
            xn32_sg[sl] = tk_hi

            dest = xn_own if t < 16 else xn_oth
            tc = (t % 16) * 128
            dest_r = dest.ap().rearrange("p (dc t) -> p dc t", dc=DC)
            for half in range(2):
                w = [tk_hi, tk_ident]
                if psa_g[half]:
                    w.append(psa_g[half])
                last = None
                for j in range(4):
                    dc = half * 4 + j
                    last = P.add("tensor", (lambda e, _s=sl, _dc=dc, _j=j,
                                            _h=half:
                        e.transpose(psa[_h][:, _j * 128:(_j + 1) * 128],
                                    hib[:, _s * D + _dc * 128:
                                        _s * D + (_dc + 1) * 128],
                                    ident[:, :])),
                        waits=w if j == 0 else [], inc="tv")
                hib_tg[sl] = last
                psa_g[half] = P.add("vector", (lambda e, _h=half, _dr=dest_r,
                                               _tc=tc: e.tensor_copy(
                    out=_dr[:, _h * 4:(_h + 1) * 4, _tc:_tc + 128],
                    in_=psa[_h][:, :])), waits=[last], inc="vv")
            tk_fm = psa_g[1]

            if t < 16:
                w = [tk_hi]
                if lob_tg:
                    w.append(lob_tg)
                tk_lo = P.add("vector", (lambda e, _s=sl: e.tensor_tensor(
                    out=lob[:, :], in0=xn32[:, _s * D:(_s + 1) * D],
                    in1=hib[:, _s * D:(_s + 1) * D], op=ALU.subtract)),
                    waits=w, inc="vv")
                hib_vg[sl] = tk_lo
                for half in range(2):
                    w = [tk_lo, psa_g[half]]
                    if lofm_tg:
                        w.append(lofm_tg)
                    last = None
                    for j in range(4):
                        dc = half * 4 + j
                        last = P.add("tensor", (lambda e, _dc=dc, _j=j,
                                                _h=half:
                            e.transpose(psa[_h][:, _j * 128:(_j + 1) * 128],
                                        lob[:, _dc * 128:(_dc + 1) * 128],
                                        ident[:, :])),
                            waits=w if j == 0 else [], inc="tv")
                    lob_tg = last
                    psa_g[half] = P.add("vector", (lambda e, _h=half:
                        e.tensor_copy(out=lofm[:, _h * 512:(_h + 1) * 512],
                                      in_=psa[_h][:, :])),
                        waits=[last], inc="vv")
                tk_lofm = psa_g[1]
                terms = []
                for dc in range(DC):
                    terms += [("xn", dc, gwhi), ("xn", dc, gwlo),
                              ("lo", dc, gwhi)]
                mml = None
                for i, (src, dc, rb) in enumerate(terms):
                    def f_mm(e, _src=src, _dc=dc, _rb=rb, _tc=tc,
                             _st=(i == 0), _sp=(i == len(terms) - 1)):
                        lhsT = (xn_own[:, _dc * TOK + _tc:
                                       _dc * TOK + _tc + 128]
                                if _src == "xn" else
                                lofm[:, _dc * 128:(_dc + 1) * 128])
                        return e.matmul(psr[:, 0:8], lhsT,
                                        _rb[:, _dc * 8:(_dc + 1) * 8],
                                        start=_st, stop=_sp)
                    w = []
                    if i == 0:
                        w = [tk_fm, tk_lofm, tk_gwlo]
                        if psr_g:
                            w.append(psr_g)
                    mml = P.add("tensor", f_mm, waits=w, inc="tv")
                lofm_tg = mml
                psr_g = P.add("vector", (lambda e, _t=t: e.tensor_copy(
                    out=logits[:, _t * 8:(_t + 1) * 8], in_=psr[:, 0:8])),
                    waits=[mml], inc="vv")
                tk_router.append(psr_g)

        if debug:
            P.add("gpsimd", (lambda e: e.dma_start(
                out=dbg["dbg_x0"], in_=xt[:, 0:D])),
                waits=[xt_vg[0], xt_sg[0]], inc="dbgo", inc_by=16)

        M1C, M2C, NM1, DENC, SCLC = 0, 16, 32, 48, 64
        gcol = lambda base, t: gb[:, base + t: base + t + 1]
        ltile = lambda t: logits[:, t * 8:(t + 1) * 8]

        for t in range(16):
            P.add("vector", (lambda e, _t=t: e.reduce_max(
                gcol(M1C, _t), ltile(_t), axis=AX.X)),
                waits=[tk_router[t]], inc="vv")
        tk_nm1 = P.add("vector", lambda e: e.tensor_scalar(
            out=gb[:, NM1:NM1 + 16], in0=gb[:, M1C:M1C + 16], scalar1=-1.0,
            scalar2=None, op0=ALU.mult), inc="vv")
        tk_m2l = None
        for t in range(16):
            P.add("vector", (lambda e, _t=t: e.tensor_scalar(
                out=pmb[:, _t * 8:(_t + 1) * 8], in0=ltile(_t),
                scalar1=gcol(M1C, _t), scalar2=None, op0=ALU.is_ge)),
                waits=[tk_nm1], inc="vv")
            P.add("vector", (lambda e, _t=t: e.scalar_tensor_tensor(
                out=pmb[:, _t * 8:(_t + 1) * 8],
                in0=pmb[:, _t * 8:(_t + 1) * 8], scalar=-1e30,
                in1=ltile(_t), op0=ALU.mult, op1=ALU.add)), inc="vv")
            tk_m2l = P.add("vector", (lambda e, _t=t: e.reduce_max(
                gcol(M2C, _t), pmb[:, _t * 8:(_t + 1) * 8], axis=AX.X)),
                inc="vv")
        tk_expl = None
        for t in range(16):
            tk_expl = P.add("scalar", (lambda e, _t=t: e.activation(
                out=pmb[:, _t * 8:(_t + 1) * 8], in_=ltile(_t), func=AF.Exp,
                bias=gcol(NM1, _t), scale=1.0)),
                waits=[tk_nm1, tk_m2l], inc="sv")
        tk_denl = None
        for t in range(16):
            P.add("vector", (lambda e, _t=t: e.tensor_scalar(
                out=scr1[:, _t * 8:(_t + 1) * 8], in0=ltile(_t),
                scalar1=gcol(M2C, _t), scalar2=None, op0=ALU.is_ge)),
                waits=[tk_m2l, tk_expl], inc="vv")
            P.add("vector", (lambda e, _t=t: e.tensor_tensor(
                out=pmb[:, _t * 8:(_t + 1) * 8],
                in0=pmb[:, _t * 8:(_t + 1) * 8],
                in1=scr1[:, _t * 8:(_t + 1) * 8], op=ALU.mult)), inc="vv")
            tk_denl = P.add("vector", (lambda e, _t=t: e.reduce_sum(
                gcol(DENC, _t), pmb[:, _t * 8:(_t + 1) * 8], axis=AX.X)),
                inc="vv")
        P.add("vector", lambda e: e.tensor_scalar(
            out=gb[:, DENC:DENC + 16], in0=gb[:, DENC:DENC + 16], scalar1=1.0,
            scalar2=1e-20, op0=ALU.mult, op1=ALU.add), waits=[tk_denl],
            inc="vv")
        P.add("vector", lambda e: e.reciprocal(
            gb[:, SCLC:SCLC + 16], gb[:, DENC:DENC + 16]), inc="vv")
        tk_scl = P.add("vector", lambda e: e.tensor_scalar(
            out=gb[:, SCLC:SCLC + 16], in0=gb[:, SCLC:SCLC + 16], scalar1=2.5,
            scalar2=None, op0=ALU.mult), inc="vv")
        tk_comb = None
        for t in range(16):
            tk_comb = P.add("vector", (lambda e, _t=t: e.tensor_scalar(
                out=comb[:, _t * 8:(_t + 1) * 8],
                in0=pmb[:, _t * 8:(_t + 1) * 8], scalar1=gcol(SCLC, _t),
                scalar2=None, op0=ALU.mult)), waits=[tk_scl], inc="vv")

        feA = P.fence()
        if debug:
            P.add("gpsimd", (lambda e: e.dma_start(
                out=dbg["dbg_xa"], in_=xn_own[:, 0:1024])), waits=feA[:2],
                inc="dbgo", inc_by=16)
            P.add("gpsimd", (lambda e: e.dma_start(
                out=dbg["dbg_st"], in_=stats[:, :])), waits=feA[:2],
                inc="dbgo", inc_by=16)
        esA.close()

        wTp = esB.enter_context(nc.sbuf_tensor("wTp", [128, DC * 384], BF16))
        kpair = esB.enter_context(nc.sbuf_tensor("kpair", [128, SEQ], BF16))
        vfm = esB.enter_context(nc.sbuf_tensor("vfm", [128, 2 * 512], BF16))
        qpair = esB.enter_context(nc.sbuf_tensor("qpair", [128, TOK], BF16))
        vp = esB.enter_context(nc.sbuf_tensor("vp", [128, 32 * 130], BF16))
        ttb = esB.enter_context(nc.sbuf_tensor("ttb", [128, 2 * SEQ], F32))
        pb = esB.enter_context(nc.sbuf_tensor("pb", [128, 4 * SEQ], BF16))
        ptb = esB.enter_context(nc.sbuf_tensor("ptb", [128, 2 * 512], BF16))
        mxb = esB.enter_context(nc.sbuf_tensor("mxb", [128, 8], F32))
        lsacc = esB.enter_context(nc.sbuf_tensor("lsacc", [128, 16], F32))
        atn = esB.enter_context(nc.sbuf_tensor("atn", [128, 64], BF16))
        atn2 = esB.enter_context(nc.sbuf_tensor("atn2", [128, 128], BF16))
        rec = esB.enter_context(nc.sbuf_tensor("rec", [128, 1], F32))
        I16 = mybir.dt.int16
        a_nat = esB.enter_context(nc.sbuf_tensor("a_nat", [128, 3968], I16))
        d2n = esB.enter_context(nc.sbuf_tensor("d2n", [128, 3968], I16))
        psS = [esB.enter_context(nc.psum_tensor("psS0", [128, 512], F32)),
               esB.enter_context(nc.psum_tensor("psS1", [128, 512], F32))]
        psAV = [esB.enter_context(nc.psum_tensor("psAV0", [128, 512], F32)),
                esB.enter_context(nc.psum_tensor("psAV1", [128, 512], F32))]
        psTM = [esB.enter_context(nc.psum_tensor("psTM0", [128, 512], BF16)),
                esB.enter_context(nc.psum_tensor("psTM1", [128, 512], BF16))]
        psFM = esB.enter_context(nc.psum_tensor("psFM", [128, 512], BF16))

        vp_dst = lambda kt0: bass.AP(vp.ap().tensor, kt0 * 130,
                                     [[32 * 130, 128], [130, 4], [65, 2],
                                      [1, 64]])
        psTM_src4 = lambda t: bass.AP(psTM[t].ap().tensor, 0,
                                      [[512, 128], [128, 4], [64, 2],
                                       [1, 64]])

        tk_ian = P.add("gpsimd", lambda e: e.iota(
            d2n[:, :], pattern=[[-1, 3968]], base=1920, channel_multiplier=1),
            waits=feA[:2], inc="gv")
        tk_an = P.add("scalar", lambda e: e.activation(
            out=a_nat[:, :], in_=d2n[:, :], func=AF.Abs, bias=0.0, scale=1.0),
            waits=[tk_ian], inc="sv")
        tk_d2 = P.add("vector", lambda e: e.tensor_scalar(
            out=d2n[:, :], in0=d2n[:, :], scalar1=svec[:, :], scalar2=None,
            op0=ALU.mult), waits=[tk_ian, tk_an, ("dsm", 16)], inc="vv")
        tk_vp1 = P.add("gpsimd", lambda e: e.memset(vp[:, :], 1.0),
                       waits=feA[:2], inc="gv")
        P.add("vector", lambda e: e.memset(ttb[:, :], 0.0), waits=feA[:1],
              inc="vv")
        tk_init = P.add("vector", lambda e: e.memset(pb[:, :], 0.0),
                        waits=feA[:1], inc="vv")

        nws = 0
        psP_g = [None, None]
        psS_g = [None, None]
        ttb_g = [None, None]
        pb_g = [None] * 4
        ptb_g = [None, None]
        psTM_g = [None, None]
        psAV_g = [None, None]
        psFM_g = None
        atn_g = None
        atn2_g = None
        rec_g = [None]
        lsacc_tree_g = [None, None]
        dbg_guard = [None]
        prev_scores = None
        vstg_g = [None, None]
        prev_av = tk_vp1
        wTp_g = None
        gslot = 0

        def aliveKT(h, kt, qt):
            ktr = kt % 16
            dlo = qt * 128 - ktr * 128 - 127
            dhi = qt * 128 + 127 - ktr * 128
            if kt < 16:
                dmin = 0 if (dlo <= 0 <= dhi) else min(abs(dlo), abs(dhi))
            else:
                dmin = 2048 - max(-dlo, dhi)
            return SLOPES[h] * dmin <= CUT

        for p in range(NPAIR):
            w = [feA[0], feA[1], feA[2]] if p == 0 else [wTp_g]
            P.add("gpsimd", (lambda e, _p=p: e.dma_start(
                out=wTp.ap().rearrange("p (dc f) -> p dc f", dc=DC),
                in_=qkvwt_d[:, _p * 384:(_p + 1) * 384]
                    .rearrange("(dc p2) f -> p2 dc f", p2=128))),
                waits=w, inc="dws", inc_by=16)
            nws += 1
            tk_wT = ("dws", 16 * nws)

            qcol = qkvb[:, 3 * p + 0: 3 * p + 1]
            kcol = qkvb[:, 3 * p + 1: 3 * p + 2]
            vcol = qkvb[:, 3 * p + 2: 3 * p + 3]

            tk_kp = None
            for nch in range(NCH):
                sl = nch % 2
                last = None
                for dc in range(DC):
                    def f_mm(e, _dc=dc, _n=nch, _sl=sl,
                             _st=(dc == 0), _sp=(dc == DC - 1)):
                        lhsT = wTp[:, _dc * 384 + 128: _dc * 384 + 256]
                        src = xn_own if _n < 4 else xn_oth
                        rhs = src[:, _dc * TOK + (_n % 4) * 512:
                                  _dc * TOK + (_n % 4) * 512 + 512]
                        return e.matmul(psS[_sl][:, :], lhsT, rhs,
                                        start=_st, stop=_sp)
                    w = []
                    if dc == 0:
                        w = [tk_wT]
                        if psP_g[sl]:
                            w.append(psP_g[sl])
                    last = P.add("tensor", f_mm, waits=w, inc="tv")
                w = [last, ("dsm", 48)]
                if prev_scores:
                    w.append(prev_scores)
                psP_g[sl] = P.add("vector", (lambda e, _n=nch, _sl=sl,
                                             _kc=kcol: e.tensor_scalar(
                    out=kpair[:, _n * 512:(_n + 1) * 512], in0=psS[_sl][:, :],
                    scalar1=_kc, scalar2=None, op0=ALU.add)),
                    waits=w, inc="vv")
                tk_kp = psP_g[sl]

            for vh in range(2):
                for nq in range(4):
                    nch = vh * 4 + nq
                    sl = nch % 2
                    last = None
                    for dc in range(DC):
                        def f_mm(e, _dc=dc, _n=nch, _sl=sl,
                                 _st=(dc == 0), _sp=(dc == DC - 1)):
                            lhsT = wTp[:, _dc * 384 + 256: _dc * 384 + 384]
                            src = xn_own if _n < 4 else xn_oth
                            rhs = src[:, _dc * TOK + (_n % 4) * 512:
                                      _dc * TOK + (_n % 4) * 512 + 512]
                            return e.matmul(psS[_sl][:, :], lhsT, rhs,
                                            start=_st, stop=_sp)
                        w = []
                        if dc == 0:
                            w = [psP_g[sl]] if psP_g[sl] else []
                        last = P.add("tensor", f_mm, waits=w, inc="tv")
                    w = [last]
                    if vstg_g[sl]:
                        w.append(vstg_g[sl])
                    psP_g[sl] = P.add("vector", (lambda e, _sl=sl,
                                                 _vc=vcol: e.tensor_scalar(
                        out=vfm[:, _sl * 512:(_sl + 1) * 512],
                        in0=psS[_sl][:, :], scalar1=_vc, scalar2=None,
                        op0=ALU.add)), waits=w, inc="vv")
                    vtok = psP_g[sl]
                    tsl = nch % 2
                    last = None
                    for j in range(4):
                        def f_tr(e, _j=j, _sl=sl, _t=tsl):
                            src = vfm[:, _sl * 512 + _j * 128:
                                      _sl * 512 + (_j + 1) * 128]
                            return e.transpose(
                                psTM[_t][:, _j * 128:(_j + 1) * 128], src,
                                ident[:, :])
                        w = []
                        if j == 0:
                            w = [vtok, tk_ident]
                            if psTM_g[tsl]:
                                w.append(psTM_g[tsl])
                        last = P.add("tensor", f_tr, waits=w, inc="tv")
                    vstg_g[sl] = last
                    kt0 = vh * 16 + nq * 4
                    w = [last, prev_av]
                    psTM_g[tsl] = P.add("vector", (lambda e, _k=kt0, _t=tsl:
                        e.tensor_copy(out=vp_dst(_k), in_=psTM_src4(_t))),
                        waits=w, inc="vv")
            tk_vp = psTM_g[1]

            tk_qp = None
            for nq in range(QCH):
                sl = nq % 2
                last = None
                for dc in range(DC):
                    def f_mm(e, _dc=dc, _n=nq, _sl=sl,
                             _st=(dc == 0), _sp=(dc == DC - 1)):
                        lhsT = wTp[:, _dc * 384 + 0: _dc * 384 + 128]
                        rhs = xn_own[:, _dc * TOK + _n * 512:
                                     _dc * TOK + _n * 512 + 512]
                        return e.matmul(psS[_sl][:, :], lhsT, rhs,
                                        start=_st, stop=_sp)
                    w = []
                    if dc == 0:
                        w = [psP_g[sl]] if psP_g[sl] else []
                    last = P.add("tensor", f_mm, waits=w, inc="tv")
                w = [last]
                if prev_scores:
                    w.append(prev_scores)
                psP_g[sl] = P.add("vector", (lambda e, _n=nq, _sl=sl,
                                             _qc2=qcol: e.tensor_scalar(
                    out=qpair[:, _n * 512:(_n + 1) * 512], in0=psS[_sl][:, :],
                    scalar1=_qc2, scalar2=0.125, op0=ALU.add, op1=ALU.mult)),
                    waits=w, inc="vv")
                tk_qp = psP_g[sl]
            psS_g = [psP_g[0], psP_g[1]]

            if debug and p == 0:
                P.add("gpsimd", (lambda e: e.dma_start(
                    out=dbg["dbg_kp"], in_=kpair[:, :])),
                    waits=[tk_kp], inc="dbgo", inc_by=16)
                P.add("gpsimd", (lambda e: e.dma_start(
                    out=dbg["dbg_q"], in_=qpair[:, :])),
                    waits=[tk_qp], inc="dbgo", inc_by=16)
                P.add("gpsimd", (lambda e: e.dma_start(
                    out=dbg["dbg_vp"], in_=vp[:, 0:4160])), waits=[tk_vp],
                    inc="dbgo", inc_by=16)

            first_score = None
            last_score = None

            def stage1(idx, hh, qt):
                nonlocal first_score, last_score
                h = 2 * p + hh
                hr = hh * 64
                half = qt % 2
                pslot = idx % 4
                kcs = [kc for kc in range(8)
                       if any(aliveKT(h, kc * 4 + j, qt) for j in range(4))]
                stt_last = None
                stt_toks = {}
                for kc in kcs:
                    s2 = (qt * 8 + kc) % 2

                    def f_sc(e, _hr=hr, _kt2=kc, _qt=qt, _s2=s2):
                        return e.matmul(
                            psS[_s2][:, :],
                            qpair[_hr:_hr + 64, _qt * 128:(_qt + 1) * 128],
                            kpair[_hr:_hr + 64, _kt2 * 512:(_kt2 + 1) * 512],
                            start=True, stop=True)
                    w = [tk_kp, tk_qp]
                    if psS_g[s2]:
                        w.append(psS_g[s2])
                    tk_s = P.add("tensor", f_sc, waits=w, inc="tv")
                    if first_score is None:
                        first_score = tk_s
                    last_score = tk_s
                    kcr = kc % 4
                    w0 = kcr * 512 - qt * 128 + 1920
                    master = a_nat if kc < 4 else d2n

                    def f_stt(e, _m=master, _w=w0, _s2=s2, _h=h, _half=half,
                              _kc=kc):
                        return e.scalar_tensor_tensor(
                            out=ttb[:, _half * SEQ + _kc * 512:
                                    _half * SEQ + (_kc + 1) * 512],
                            in0=_m[:, _w:_w + 512], scalar=-SLOPES[_h],
                            in1=psS[_s2][:, :], op0=ALU.mult, op1=ALU.add)
                    w = [tk_s, tk_an, tk_d2, tk_init]
                    if ttb_g[half]:
                        w.append(ttb_g[half])
                    psS_g[s2] = P.add("vector", f_stt, waits=w, inc="vv")
                    stt_last = psS_g[s2]
                    stt_toks[kc] = stt_last
                runs = []
                for kc in kcs:
                    if runs and runs[-1][1] == kc and (kc != 4):
                        runs[-1][1] = kc + 1
                    else:
                        runs.append([kc, kc + 1])
                lbase = half * 8
                lse_toks = []
                for ri, (k0, k1) in enumerate(runs):
                    lb = cbias(0.0 if k0 < 4 else -256.0 * SLOPES[h])
                    w = [stt_toks[k1 - 1], cst_tok[0]]
                    if ri == 0:
                        if lsacc_tree_g[half]:
                            w.append(lsacc_tree_g[half])
                        if pb_g[pslot]:
                            w.append(pb_g[pslot])
                    lse_toks.append(P.add("scalar", (lambda e, _k0=k0, _k1=k1,
                                                     _half=half, _ri=ri,
                                                     _b=lb, _ps=pslot:
                        e.activation(
                            out=pb[:, _ps * SEQ + _k0 * 512:
                                   _ps * SEQ + _k1 * 512],
                            in_=ttb[:, _half * SEQ + _k0 * 512:
                                    _half * SEQ + _k1 * 512],
                            func=AF.Exp, bias=_b, scale=0.125,
                            accum_out=lsacc[:, _half * 8 + _ri:
                                            _half * 8 + _ri + 1])),
                        waits=w, inc="sv"))
                if len(runs) > 1:
                    tk_sum = P.add("vector", (lambda e, _lb=lbase:
                        e.tensor_tensor(
                            out=lsacc[:, _lb:_lb + 1],
                            in0=lsacc[:, _lb:_lb + 1],
                            in1=lsacc[:, _lb + 1:_lb + 2], op=ALU.add)),
                        waits=[lse_toks[-1]], inc="vv")
                else:
                    tk_sum = lse_toks[0]
                lsacc_tree_g[half] = tk_sum
                tk_ln = P.add("scalar", (lambda e, _lb=lbase: e.activation(
                    out=mxb[:, 0:1], in_=lsacc[:, _lb:_lb + 1], func=AF.Ln,
                    bias=0.0, scale=1.0)), waits=[tk_sum], inc="sv")
                tk_n1 = P.add("scalar", (lambda e: e.activation(
                    out=mxb[:, 4:5], in_=mxb[:, 0:1], func=AF.Identity,
                    bias=0.0, scale=-8.0)),
                    waits=[tk_ln] + ([dbg_guard[0]] if dbg_guard[0] else []),
                    inc="sv")
                ob = cbias(-2048.0 * SLOPES[h])
                tk_n2 = P.add("scalar", (lambda e, _b=ob: e.activation(
                    out=mxb[:, 5:6], in_=mxb[:, 0:1], func=AF.Identity,
                    bias=_b, scale=-8.0)), waits=[tk_n1, cst_tok[0]],
                    inc="sv")
                exp_last = None
                for (k0, k1) in runs:
                    bc = 4 if k0 < 4 else 5

                    def f_exp(e, _k0=k0, _k1=k1, _half=half, _bc=bc,
                              _ps=pslot):
                        return e.activation(
                            out=pb[:, _ps * SEQ + _k0 * 512:
                                   _ps * SEQ + _k1 * 512],
                            in_=ttb[:, _half * SEQ + _k0 * 512:
                                    _half * SEQ + _k1 * 512],
                            func=AF.Exp, bias=mxb[:, _bc:_bc + 1], scale=1.0)
                    w = [tk_n2]
                    if pb_g[pslot]:
                        w.append(pb_g[pslot])
                    exp_last = P.add("scalar", f_exp, waits=w, inc="sv")
                ttb_g[half] = exp_last
                return kcs, exp_last

            def stage2(idx, hh, qt, exp_last):
                pslot = idx % 4
                if debug and p == 0 and hh == 0 and qt == 2:
                    P.add("gpsimd", (lambda e: e.dma_start(
                        out=dbg["dbg_tt"], in_=ttb[:, 0:SEQ])),
                        waits=[exp_last], inc="dbgo", inc_by=16)
                    P.add("gpsimd", (lambda e: e.dma_start(
                        out=dbg["dbg_pb"], in_=pb[:, 0:SEQ])),
                        waits=[exp_last], inc="dbgo", inc_by=16)
                    tk_dbg = P.add("gpsimd", (lambda e: e.dma_start(
                        out=dbg["dbg_mx"][:, 4:6], in_=mxb[:, 4:6])),
                        waits=[exp_last], inc="dbgo", inc_by=16)
                    ttb_g[qt % 2] = tk_dbg
                    pb_g[pslot] = tk_dbg
                    dbg_guard[0] = tk_dbg
                nonlocal prev_av, psFM_g, atn_g, atn2_g, gslot, nsm
                h = 2 * p + hh
                half = qt % 2
                avb = qt % 2
                kts = [kt for kt in range(32) if aliveKT(h, kt, qt)]
                groups = sorted(set(kt // 4 for kt in kts))
                av_first = True
                pend_av = None
                for gi, g in enumerate(groups):
                    gsl = gslot % 2
                    gslot += 1
                    last = None
                    first_in_g = True
                    for j in range(4):
                        kt = g * 4 + j
                        if kt not in kts:
                            continue

                        def f_tr(e, _j=j, _kt=kt, _t=gsl, _ps=pslot):
                            return e.transpose(
                                psTM[_t][:, _j * 128:(_j + 1) * 128],
                                pb[:, _ps * SEQ + _kt * 128:
                                   _ps * SEQ + (_kt + 1) * 128],
                                ident[:, :])
                        w = []
                        if first_in_g:
                            w = [exp_last]
                            if psTM_g[gsl]:
                                w.append(psTM_g[gsl])
                            first_in_g = False
                        last = P.add("tensor", f_tr, waits=w, inc="tv")
                    pb_g[pslot] = last
                    w = [last]
                    if ptb_g[gsl]:
                        w.append(ptb_g[gsl])
                    if gi % 2 == 0:
                        tk_ev = P.add("vector", (lambda e, _t=gsl:
                            e.tensor_copy(out=ptb[:, _t * 512:(_t + 1) * 512],
                                          in_=psTM[_t][:, :])),
                            waits=w, inc="vv")
                    else:
                        tk_ev = P.add("scalar", (lambda e, _t=gsl:
                            e.activation(out=ptb[:, _t * 512:(_t + 1) * 512],
                                         in_=psTM[_t][:, :], func=AF.Identity,
                                         bias=0.0, scale=1.0)),
                            waits=w, inc="sv")
                    psTM_g[gsl] = tk_ev

                    def emit_avs(g2, gsl2, ev_tok, is_last_group):
                        nonlocal av_first, prev_av
                        for j2 in range(4):
                            kt2 = g2 * 4 + j2
                            if kt2 not in kts:
                                continue

                            def f_av(e, _j=j2, _kt=kt2, _hh=hh, _b=avb,
                                     _t=gsl2, _st=av_first,
                                     _sp=(kt2 == kts[-1])):
                                return e.matmul(
                                    psAV[_b][0:128, 0:65],
                                    ptb[:, _t * 512 + _j * 128:
                                        _t * 512 + _j * 128 + 128],
                                    vp[:, _kt * 130 + _hh * 65:
                                       _kt * 130 + _hh * 65 + 65],
                                    start=_st, stop=_sp)
                            w2 = [ev_tok, tk_vp]
                            if av_first and psAV_g[avb]:
                                w2.append(psAV_g[avb])
                            tk_avm = P.add("tensor", f_av, waits=w2, inc="tv")
                            ptb_g[gsl2] = tk_avm
                            prev_av = tk_avm
                            av_first = False

                    if pend_av is not None:
                        emit_avs(pend_av[0], pend_av[1], pend_av[2], False)
                    pend_av = (g, gsl, tk_ev)
                emit_avs(pend_av[0], pend_av[1], pend_av[2], True)
                av_last = prev_av
                w = [av_last]
                if rec_g[0]:
                    w.append(rec_g[0])
                tk_rec = P.add("vector", (lambda e, _b=avb: e.reciprocal(
                    rec[:, :], psAV[_b][:, 64:65])), waits=w, inc="vv")
                w = [tk_rec]
                if atn_g:
                    w.append(atn_g)
                tk_at = P.add("scalar", (lambda e, _b=avb: e.activation(
                    out=atn[:, :], in_=psAV[_b][:, 0:64], func=AF.Identity,
                    bias=0.0, scale=rec[:, :])), waits=w, inc="sv")
                psAV_g[avb] = tk_at
                rec_g[0] = tk_at
                w = [tk_at]
                if psFM_g:
                    w.append(psFM_g)
                tk_tr = P.add("tensor", (lambda e: e.transpose(
                    psFM[0:64, 0:128], atn[:, :], ident[:, :])), waits=w,
                    inc="tv")
                atn_g = tk_tr
                cc = h // 2
                if hh == 0:
                    psFM_g = P.add("vector", (lambda e, _cc=cc, _qt=qt:
                        e.tensor_copy(
                            out=attno[0:64, _cc * TOK + _qt * 128:
                                      _cc * TOK + _qt * 128 + 128],
                            in_=psFM[0:64, 0:128])), waits=[tk_tr], inc="vv")
                else:
                    w = [tk_tr]
                    if atn2_g:
                        w.append(atn2_g)
                    ev = P.add("vector", (lambda e: e.tensor_copy(
                        out=atn2[0:64, :], in_=psFM[0:64, 0:128])),
                        waits=w, inc="vv")
                    psFM_g = ev
                    P.add("sync", (lambda e, _cc=cc, _qt=qt: e.dma_start(
                        out=attno[64:128, _cc * TOK + _qt * 128:
                                  _cc * TOK + _qt * 128 + 128],
                        in_=atn2[0:64, :])), waits=[ev], inc="dsm",
                        inc_by=16)
                    nsm += 1
                    atn2_g = ("dsm", 16 * nsm)

            items = [(hh, qt) for hh in range(2) for qt in range(16)]
            pend = []
            for idx, it in enumerate(items):
                info = stage1(idx, *it)
                pend.append((idx, it, info))
                if len(pend) > 3:
                    i2, it2, inf2 = pend.pop(0)
                    stage2(i2, it2[0], it2[1], inf2[1])
            for i2, it2, inf2 in pend:
                stage2(i2, it2[0], it2[1], inf2[1])

            prev_scores = last_score
            wTp_g = first_score

        feB = P.fence()
        feB_dma = ("dsm", 16 * nsm)

        if debug:
            P.add("gpsimd", (lambda e: e.dma_start(
                out=dbg["dbg_xn"], in_=xn_own[:, 0:2048])), waits=feB[:2],
                inc="dbgo", inc_by=16)
            P.add("gpsimd", (lambda e: e.dma_start(
                out=dbg["dbg_logits"], in_=logits[:, :])), waits=feB[:2],
                inc="dbgo", inc_by=16)
            P.add("gpsimd", (lambda e: e.dma_start(
                out=dbg["dbg_comb"], in_=comb[:, :])), waits=feB[:2],
                inc="dbgo", inc_by=16)
            P.add("gpsimd", (lambda e: e.dma_start(
                out=dbg["dbg_attn"], in_=attno[:, 0:2048])),
                waits=feB[:2] + [feB_dma], inc="dbgo", inc_by=16)
            P.add("gpsimd", (lambda e: e.dma_start(
                out=dbg["dbg_attnf"], in_=attno[:, :])),
                waits=feB[:2] + [feB_dma], inc="dbgo", inc_by=16)
        esB.close()

        y_acc = es.enter_context(nc.sbuf_tensor("y_acc", [128, 16 * 1024],
                                                F32))
        esD = ExitStack()
        gate = esD.enter_context(nc.sbuf_tensor("gate", [128, DC * FF], BF16))
        up = esD.enter_context(nc.sbuf_tensor("up", [128, DC * FF], BF16))
        gq = esD.enter_context(nc.sbuf_tensor("gq", [128, FC * 512], BF16))
        dng = esD.enter_context(nc.sbuf_tensor("dng", [128, 2 * 1024], BF16))
        slb = esD.enter_context(nc.sbuf_tensor("slb", [128, 2 * 512], BF16))
        psD = [esD.enter_context(nc.psum_tensor("psD%d" % i, [128, 512], F32))
               for i in range(8)]

        nwd = 0
        gate_free = [feB[0], feB[1], feB[2], feB[3]]
        up_free = list(gate_free)
        ph1_bank_g = [None] * 4
        ph2_bank_g = [None] * 8
        slb_g = [None, None]
        dng_g = [None, None]
        gq_g = None
        last_ph1_mm = None

        for ei in range(NEXP):
            gsrc = swg_d[ei] if ei < 2 else wg_d[ei - 2]
            usrc = swu_d[ei] if ei < 2 else wu_d[ei - 2]

            P.add("gpsimd", (lambda e, _s=gsrc: e.dma_start(
                out=gate.ap().rearrange("p (dc f) -> p dc f", dc=DC),
                in_=_s.rearrange("(dc p) f -> p dc f", p=128))),
                waits=gate_free, inc="dws", inc_by=16)
            nws += 1
            tk_gdma = ("dws", 16 * nws)
            P.add("gpsimd", (lambda e, _s=usrc: e.dma_start(
                out=up.ap().rearrange("p (dc f) -> p dc f", dc=DC),
                in_=_s.rearrange("(dc p) f -> p dc f", p=128))),
                waits=up_free, inc="dws", inc_by=16)
            nws += 1
            tk_udma = ("dws", 16 * nws)

            for q in range(4):
                gq_toks = []
                for fc in range(FC):
                    bg = fc % 2
                    bu = 2 + fc % 2
                    lastg = None
                    for dc in range(DC):
                        def f_mm(e, _dc=dc, _fc=fc, _q=q, _b=bg,
                                 _st=(dc == 0), _sp=(dc == DC - 1)):
                            return e.matmul(
                                psD[_b][:, :],
                                gate[:, _dc * FF + _fc * 128:
                                     _dc * FF + (_fc + 1) * 128],
                                xn_own[:, _dc * TOK + _q * 512:
                                       _dc * TOK + _q * 512 + 512],
                                start=_st, stop=_sp)
                        w = []
                        if dc == 0:
                            w = [tk_gdma, feB[1], feB[2]]
                            if ph1_bank_g[bg]:
                                w.append(ph1_bank_g[bg])
                        lastg = P.add("tensor", f_mm, waits=w, inc="tv")
                    lastu = None
                    for dc in range(DC):
                        def f_mm(e, _dc=dc, _fc=fc, _q=q, _b=bu,
                                 _st=(dc == 0), _sp=(dc == DC - 1)):
                            return e.matmul(
                                psD[_b][:, :],
                                up[:, _dc * FF + _fc * 128:
                                   _dc * FF + (_fc + 1) * 128],
                                xn_own[:, _dc * TOK + _q * 512:
                                       _dc * TOK + _q * 512 + 512],
                                start=_st, stop=_sp)
                        w = []
                        if dc == 0:
                            w = [tk_udma]
                            if ph1_bank_g[bu - 2 + 2]:
                                w.append(ph1_bank_g[bu - 2 + 2] if False
                                         else ph1_bank_g[bu] if bu < 4 else None)
                        w = [x for x in w if x]
                        lastu = P.add("tensor", f_mm, waits=w, inc="tv")
                    last_ph1_mm = lastu
                    ssl = fc % 2
                    w = [lastg]
                    if slb_g[ssl]:
                        w.append(slb_g[ssl])
                    tk_sl = P.add("scalar", (lambda e, _b=bg, _s=ssl:
                        e.activation(out=slb[:, _s * 512:(_s + 1) * 512],
                                     in_=psD[_b][:, :], func=AF.Silu,
                                     bias=0.0, scale=1.0)), waits=w, inc="sv")
                    ph1_bank_g[bg] = tk_sl
                    w = [tk_sl, lastu]
                    if gq_g:
                        w.append(gq_g)
                    tk_g = P.add("vector", (lambda e, _b=bu, _s=ssl, _fc=fc:
                        e.tensor_tensor(
                            out=gq[:, _fc * 512:(_fc + 1) * 512],
                            in0=psD[_b][:, :],
                            in1=slb[:, _s * 512:(_s + 1) * 512],
                            op=ALU.mult)), waits=w, inc="vv")
                    ph1_bank_g[bu] = tk_g
                    slb_g[ssl] = tk_g
                    gq_toks.append(tk_g)

                mm_last = {}
                dsl_last = None
                for fc in range(FC):
                    dsl = fc % 2
                    dslice = (swd_d[ei] if ei < 2 else wd_d[ei - 2])
                    w = [dng_g[dsl]] if dng_g[dsl] else []
                    P.add("gpsimd", (lambda e, _fc=fc, _sl=dsl, _ds=dslice:
                        e.dma_start(
                            out=dng[:, _sl * 1024:(_sl + 1) * 1024],
                            in_=_ds[_fc * 128:(_fc + 1) * 128, :])),
                        waits=w, inc="dwd", inc_by=16)
                    nwd += 1
                    tk_d = ("dwd", 16 * nwd)
                    for tt in range(4):
                        for dh in range(2):
                            b = tt * 2 + dh

                            def f_mm(e, _fc=fc, _tt=tt, _dh=dh, _b=b,
                                     _sl=dsl, _st=(fc == 0),
                                     _sp=(fc == FC - 1)):
                                return e.matmul(
                                    psD[_b][:, :],
                                    gq[:, _fc * 512 + _tt * 128:
                                       _fc * 512 + _tt * 128 + 128],
                                    dng[:, _sl * 1024 + _dh * 512:
                                        _sl * 1024 + _dh * 512 + 512],
                                    start=_st, stop=_sp)
                            w = [tk_d, gq_toks[fc]]
                            if fc == 0:
                                if ph2_bank_g[b]:
                                    w.append(ph2_bank_g[b])
                                if b < 4 and ph1_bank_g[b]:
                                    w.append(ph1_bank_g[b])
                            mm_last[b] = P.add("tensor", f_mm, waits=w,
                                               inc="tv")
                    dng_g[dsl] = mm_last[7]
                gq_g = mm_last[7]
                for tt in range(4):
                    for dh in range(2):
                        b = tt * 2 + dh
                        ttg_ = q * 4 + tt
                        yap = (lambda _t=ttg_, _dh=dh: y_acc[
                            :, _t * 1024 + _dh * 512:
                            _t * 1024 + _dh * 512 + 512])
                        if ei == 0:
                            fn = (lambda e, _b=b, _y=yap: e.tensor_copy(
                                out=_y(), in_=psD[_b][:, :]))
                        elif ei == 1:
                            fn = (lambda e, _b=b, _y=yap: e.tensor_tensor(
                                out=_y(), in0=psD[_b][:, :], in1=_y(),
                                op=ALU.add))
                        else:
                            csl = comb[:, ttg_ * 8 + (ei - 2):
                                       ttg_ * 8 + (ei - 2) + 1]
                            fn = (lambda e, _b=b, _y=yap, _c=csl:
                                  e.scalar_tensor_tensor(
                                      out=_y(), in0=psD[_b][:, :], scalar=_c,
                                      in1=_y(), op0=ALU.mult, op1=ALU.add))
                        ph2_bank_g[b] = P.add("vector", fn,
                                              waits=[mm_last[b], tk_comb],
                                              inc="vv")
                for b in range(4):
                    ph1_bank_g[b] = ph2_bank_g[b]
            gate_free = [last_ph1_mm]
            up_free = [last_ph1_mm]

        feD = P.fence()
        esD.close()

        if debug:
            P.add("gpsimd", (lambda e: e.dma_start(
                out=dbg["dbg_y"], in_=y_acc[:, 0:1024])), waits=feD[:2],
                inc="dbgo", inc_by=16)

        esE = ExitStack()
        outwT = esE.enter_context(nc.sbuf_tensor("outwT", [128, DC * D], BF16))
        bout = esE.enter_context(nc.sbuf_tensor("bout", [128, D], F32))
        xe = esE.enter_context(nc.sbuf_tensor("xe", [128, 2 * D], F32))
        ssb = esE.enter_context(nc.sbuf_tensor("ssb", [128, D], F32))
        gsb = esE.enter_context(nc.sbuf_tensor("gsb", [128, D], F32))
        osb = esE.enter_context(nc.sbuf_tensor("osb", [128, 2 * D], F32))
        psE = [esE.enter_context(nc.psum_tensor("psE0", [128, 512], F32)),
               esE.enter_context(nc.psum_tensor("psE1", [128, 512], F32))]

        P.add("gpsimd", (lambda e: e.dma_start(
            out=outwT.ap().rearrange("p (cc o) -> p cc o", cc=DC),
            in_=outwt_d.rearrange("(cc p) o -> p cc o", p=128))),
            waits=feD[:3], inc="dws", inc_by=16)
        nws += 1
        tk_ow = ("dws", 16 * nws)
        tk_bo = P.add("sync", (lambda e: e.dma_start(
            out=bout[:, :], in_=outb_d)), waits=feD[:3], inc="dsm",
            inc_by=16)
        nsm += 1

        nxe = 0
        xe_g = [None, None]
        osb_g = [None, None]
        gsb_g = [None, None]
        psE_g = [None, None]
        for t in range(16):
            sl = t % 2
            w = [x for x in (xe_g[sl], osb_g[sl]) if x] or feD[:1]
            P.add("sync", (lambda e, _t=t, _sl=sl: e.dma_start(
                out=xe[:, _sl * D:(_sl + 1) * D],
                in_=x_d[_t * 128:(_t + 1) * 128, :])), waits=w, inc="dxe",
                inc_by=16)
            nxe += 1
            tk_x = ("dxe", 16 * nxe)
            tk_o = None
            for dh in range(2):
                b = dh
                last = None
                for cc in range(DC):
                    def f_mm(e, _cc=cc, _t=t, _dh=dh, _b=b,
                             _st=(cc == 0), _sp=(cc == DC - 1)):
                        return e.matmul(
                            psE[_b][:, :],
                            attno[:, _cc * TOK + _t * 128:
                                  _cc * TOK + _t * 128 + 128],
                            outwT[:, _cc * D + _dh * 512:
                                  _cc * D + _dh * 512 + 512],
                            start=_st, stop=_sp)
                    w = []
                    if cc == 0:
                        w = [tk_ow, feB[0], feB[1], feB[2], feB_dma]
                        if psE_g[b]:
                            w.append(psE_g[b])
                    last = P.add("tensor", f_mm, waits=w, inc="tv")
                tk_s = P.add("vector", (lambda e, _b=b, _dh=dh:
                    e.tensor_tensor(
                        out=ssb[:, _dh * 512:(_dh + 1) * 512],
                        in0=psE[_b][:, :],
                        in1=bout[:, _dh * 512:(_dh + 1) * 512], op=ALU.add)),
                    waits=[last, tk_bo], inc="vv")
                psE_g[b] = tk_s
                w = [feD[1]]
                if gsb_g[dh]:
                    w.append(gsb_g[dh])
                tk_gg = P.add("scalar", (lambda e, _t=t, _dh=dh: e.activation(
                    out=gsb[:, _dh * 512:(_dh + 1) * 512],
                    in_=y_acc[:, _t * 1024 + _dh * 512:
                              _t * 1024 + _dh * 512 + 512],
                    func=AF.Silu, bias=0.0, scale=1.0)),
                    waits=w, inc="sv")
                tk_sg = P.add("vector", (lambda e, _dh=dh: e.tensor_tensor(
                    out=gsb[:, _dh * 512:(_dh + 1) * 512],
                    in0=ssb[:, _dh * 512:(_dh + 1) * 512],
                    in1=gsb[:, _dh * 512:(_dh + 1) * 512], op=ALU.mult)),
                    waits=[tk_gg], inc="vv")
                w = [tk_x]
                if dh == 0 and osb_g[sl]:
                    w.append(osb_g[sl])
                tk_o = P.add("vector", (lambda e, _sl=sl, _dh=dh:
                    e.tensor_tensor(
                        out=osb[:, _sl * D + _dh * 512:
                                _sl * D + _dh * 512 + 512],
                        in0=gsb[:, _dh * 512:(_dh + 1) * 512],
                        in1=xe[:, _sl * D + _dh * 512:
                               _sl * D + _dh * 512 + 512],
                        op=ALU.add)), waits=w, inc="vv")
                gsb_g[dh] = tk_o
            xe_g[sl] = tk_o
            osb_g[sl] = P.add("sync", (lambda e, _t=t, _sl=sl: e.dma_start(
                out=out_d[_t * 128:(_t + 1) * 128, :],
                in_=osb[:, _sl * D:(_sl + 1) * D])), waits=[tk_o],
                inc="dout", inc_by=16)
        total_out = P.cnt.get("dout", 0)
        total_dbg = P.cnt.get("dbgo", 0)
        fin = [("dout", total_out)]
        if total_dbg:
            fin.append(("dbgo", total_dbg))
        P.add("sync", (lambda e: e.nop()), waits=fin)
        esE.close()

        with nc.Block() as block:
            P.emit(block, semh)

    return nc


_CACHE = {}


def _perm():
    idx = []
    for p_ in range(NPAIR):
        for w in range(3):
            for hh in range(2):
                h = 2 * p_ + hh
                base = h * 192 + w * 64
                idx.extend(range(base, base + 64))
    return np.array(idx, dtype=np.int64)


def kernel(L0_attn_x, L0_attn_qkv_w, L0_attn_qkv_b, L0_attn_out_w,
           L0_attn_out_b, L0_ffn_gate_w, L0_ffn_w_gate, L0_ffn_w_up,
           L0_ffn_w_down, L0_ffn_sw_gate, L0_ffn_sw_up, L0_ffn_sw_down):
    if "nc" not in _CACHE:
        _CACHE["nc"] = build(debug=False)
    nc = _CACHE["nc"]

    perm = _perm()
    f32 = np.float32
    qkvwt = np.ascontiguousarray(np.asarray(L0_attn_qkv_w, f32)[perm].T)
    qkvb = np.ascontiguousarray(
        np.asarray(L0_attn_qkv_b, f32)[perm].reshape(24, 128).T)
    outwt = np.ascontiguousarray(np.asarray(L0_attn_out_w, f32).T)
    outb = np.ascontiguousarray(
        np.broadcast_to(np.asarray(L0_attn_out_b, f32), (128, D)))
    gwt = np.ascontiguousarray(np.asarray(L0_ffn_gate_w, f32).T)
    wg = np.ascontiguousarray(np.asarray(L0_ffn_w_gate, f32))
    wu = np.ascontiguousarray(np.asarray(L0_ffn_w_up, f32))
    wd = np.ascontiguousarray(np.asarray(L0_ffn_w_down, f32))
    swg = np.ascontiguousarray(
        np.asarray(L0_ffn_sw_gate, f32).reshape(D, 2, FF).transpose(1, 0, 2))
    swu = np.ascontiguousarray(
        np.asarray(L0_ffn_sw_up, f32).reshape(D, 2, FF).transpose(1, 0, 2))
    swd = np.ascontiguousarray(
        np.asarray(L0_ffn_sw_down, f32).reshape(2, FF, D))
    x = np.asarray(L0_attn_x, f32)

    in_maps = []
    for c in range(8):
        b, half = c // 2, c % 2
        own = x[b, half * TOK:(half + 1) * TOK]
        oth = x[b, (1 - half) * TOK:(2 - half) * TOK]
        xcore = np.ascontiguousarray(np.concatenate([own, oth], axis=0))
        svec = np.full((128, 1), 1.0 if half == 1 else -1.0, f32)
        in_maps.append({
            "x": xcore, "svec": svec, "qkvwt": qkvwt, "qkvb": qkvb,
            "outwt": outwt, "outb": outb, "gwt": gwt, "wg": wg, "wu": wu,
            "wd": wd, "swg": swg, "swu": swu, "swd": swd,
        })

    res = run_bass_kernel_spmd(nc, in_maps, core_ids=list(range(8)))
    full = np.empty((4, SEQ, D), np.float32)
    for c in range(8):
        b, half = c // 2, c % 2
        full[b, half * TOK:(half + 1) * TOK] = res.results[c]["out"]
    return full


# version 12 (final)
# speedup vs baseline: 1.1471x; 1.1471x over previous
import numpy as np
from contextlib import ExitStack

import concourse.bass as bass
import concourse.mybir as mybir
from concourse.bass_utils import run_bass_kernel_spmd

F32 = mybir.dt.float32
BF16 = mybir.dt.bfloat16
AF = mybir.ActivationFunctionType
ALU = mybir.AluOpType
AX = mybir.AxisListType

D = 1024
H = 16
SEQ = 4096
TOK = 2048
NPAIR = 8
DC = 8
NCH = 8
QCH = 4
E_MOE = 8
FF = 1408
FC = 11
NEXP = 10
SHIFT = 16.0
CUT = 100.0
SLOPES = [0.5 ** h for h in range(H)]
LN_EPS = 1e-5
NSLOT = 8
ENGINES = ("sync", "gpsimd", "vector", "scalar", "tensor")


def _alive(h, kt, qc):
    ktr = kt % 16
    dlo = qc * 512 - ktr * 128 - 127
    dhi = qc * 512 + 511 - ktr * 128
    if kt < 16:
        dmin = 0 if (dlo <= 0 <= dhi) else min(abs(dlo), abs(dhi))
    else:
        dmin = 2048 - max(-dlo, dhi)
    return SLOPES[h] * dmin <= CUT


class Prog:
    def __init__(self):
        self.ops = {e: [] for e in ENGINES}
        self.cnt = {}

    def add(self, eng, fn, waits=(), inc=None, inc_by=1):
        ws = tuple((s, v) for sv_ in waits if sv_ is not None for (s, v) in [sv_]
                   if (s is not None) and v > 0)
        self.ops[eng].append((fn, ws, inc, inc_by))
        if inc is not None:
            self.cnt[inc] = self.cnt.get(inc, 0) + inc_by
            return (inc, self.cnt[inc])
        return None

    def fence(self, sems=("tv", "vv", "sv", "gv")):
        return [(s, self.cnt.get(s, 0)) for s in sems]

    def emit(self, block, semh):
        for ename in ENGINES:
            ops = self.ops[ename]

            def body(eng, _ops=ops, _ename=ename):
                sep = _ename in ("vector", "scalar")
                hw = {}
                for fn, waits, inc, inc_by in _ops:
                    if sep:
                        for _ in range(6):
                            eng.nop()
                    for (s, v) in waits:
                        if hw.get(s, 0) < v:
                            eng.wait_ge(semh[s], v)
                            hw[s] = v
                    instr = fn(eng)
                    if inc is not None:
                        instr.then_inc(semh[inc], inc_by)

            getattr(block, ename)(body)


def build(debug=False):
    nc = bass.Bass("TRN2", target_bir_lowering=False,
                   detect_race_conditions=False)

    x_d = nc.dram_tensor("x", [SEQ, D], F32, kind="ExternalInput").ap()
    svec_d = nc.dram_tensor("svec", [128, 1], F32, kind="ExternalInput").ap()
    qkvwt_d = nc.dram_tensor("qkvwt", [D, 3072], F32, kind="ExternalInput").ap()
    qkvb_d = nc.dram_tensor("qkvb", [128, 24], F32, kind="ExternalInput").ap()
    outwt_d = nc.dram_tensor("outwt", [D, D], F32, kind="ExternalInput").ap()
    outb_d = nc.dram_tensor("outb", [128, D], F32, kind="ExternalInput").ap()
    gwt_d = nc.dram_tensor("gwt", [D, 8], F32, kind="ExternalInput").ap()
    wg_d = nc.dram_tensor("wg", [E_MOE, D, FF], F32, kind="ExternalInput").ap()
    wu_d = nc.dram_tensor("wu", [E_MOE, D, FF], F32, kind="ExternalInput").ap()
    wd_d = nc.dram_tensor("wd", [E_MOE, FF, D], F32, kind="ExternalInput").ap()
    swg_d = nc.dram_tensor("swg", [2, D, FF], F32, kind="ExternalInput").ap()
    swu_d = nc.dram_tensor("swu", [2, D, FF], F32, kind="ExternalInput").ap()
    swd_d = nc.dram_tensor("swd", [2, FF, D], F32, kind="ExternalInput").ap()
    out_d = nc.dram_tensor("out", [TOK, D], F32, kind="ExternalOutput").ap()
    dbg = {}
    if debug:
        for name, shp in [("dbg_xn", [128, 2048]), ("dbg_logits", [128, 128]),
                          ("dbg_comb", [128, 128]), ("dbg_kp", [128, 4096]),
                          ("dbg_attn", [128, 2048]), ("dbg_y", [128, 1024]),
                          ("dbg_q", [128, 2048]), ("dbg_vp", [128, 4160]),
                          ("dbg_ptf", [128, 16384]), ("dbg_sh", [2, 2048]),
                          ("dbg_tt", [128, 4096]), ("dbg_pb", [128, 4096]),
                          ("dbg_mx", [128, 8]),
                          ("dbg_attnf", [128, 16384]),
                          ("dbg_x0", [128, 1024]), ("dbg_xa", [128, 1024]),
                          ("dbg_st", [128, 192])]:
            dbg[name] = nc.dram_tensor(name, shp, F32, kind="ExternalOutput").ap()

    P = Prog()
    SEMNAMES = ["dxa", "dxe", "dws", "dwd", "dsm", "dout", "dbgo",
                "tv", "vv", "sv", "gv"]

    with ExitStack() as es:
        semh = {s: es.enter_context(nc.semaphore(s)) for s in SEMNAMES}

        ident = es.enter_context(nc.sbuf_tensor("ident", [128, 128], BF16))
        svec = es.enter_context(nc.sbuf_tensor("svec_s", [128, 1], F32))
        qkvb = es.enter_context(nc.sbuf_tensor("qkvb_s", [128, 24], F32))
        gw32 = es.enter_context(nc.sbuf_tensor("gw32", [128, 64], F32))
        gwhi = es.enter_context(nc.sbuf_tensor("gwhi", [128, 64], BF16))
        gwlo = es.enter_context(nc.sbuf_tensor("gwlo", [128, 64], BF16))
        comb = es.enter_context(nc.sbuf_tensor("comb", [128, 128], F32))
        xn_own = es.enter_context(nc.sbuf_tensor("xn_own", [128, DC * TOK], BF16))
        attno = es.enter_context(nc.sbuf_tensor("attno", [128, DC * TOK], BF16))
        cst = es.enter_context(nc.sbuf_tensor("cst", [128, 24], F32))

        constvals = {}
        cst_tok = [None]

        def cbias(v):
            v = float(v)
            if v == 0.0:
                return v
            if v not in constvals:
                i = len(constvals)
                assert i < 24
                constvals[v] = i
                cst_tok[0] = P.add("gpsimd", (lambda e, _i=i, _v=v: e.memset(
                    cst[:, _i:_i + 1], _v)), inc="gv")
                nc.const_aps.register(F32, v, cst[:, i:i + 1])
            return v

        esB = ExitStack()
        xn_oth = esB.enter_context(nc.sbuf_tensor("xn_oth", [128, DC * TOK], BF16))

        tk_im = P.add("gpsimd", lambda e: e.memset(ident[:, :], 0.0),
                      inc="gv")
        tk_ident = P.add("gpsimd", lambda e: e.affine_select(
            out=ident[:, :], in_=ident[:, :], compare_op=ALU.not_equal,
            fill=1.0, base=0, pattern=[[-1, 128]], channel_multiplier=1),
            waits=[tk_im], inc="gv")

        P.add("sync", lambda e: e.dma_start(out=svec[:, :], in_=svec_d),
              inc="dsm", inc_by=16)
        P.add("sync", lambda e: e.dma_start(out=qkvb[:, :], in_=qkvb_d),
              inc="dsm", inc_by=16)
        P.add("sync", lambda e: e.dma_start(
            out=gw32.ap().rearrange("p (dc e) -> p dc e", dc=DC),
            in_=gwt_d.rearrange("(dc p) e -> p dc e", p=128)),
            inc="dsm", inc_by=16)
        nsm = 3

        tk_gwhi = P.add("scalar", lambda e: e.activation(
            out=gwhi[:, :], in_=gw32[:, :], func=AF.Identity, bias=0.0,
            scale=1.0), waits=[("dsm", 48)], inc="sv")
        tk_gwlo = P.add("vector", lambda e: e.tensor_tensor(
            out=gwlo[:, :], in0=gw32[:, :], in1=gwhi[:, :], op=ALU.subtract),
            waits=[tk_gwhi], inc="vv")

        esA = ExitStack()
        esG = es if debug else esA
        stats = esG.enter_context(nc.sbuf_tensor("stats", [128, 192], F32))
        logits = esG.enter_context(nc.sbuf_tensor("logits", [128, 128], F32))
        gb = esG.enter_context(nc.sbuf_tensor("gb", [128, 96], F32))
        pmb = esG.enter_context(nc.sbuf_tensor("pmb", [128, 128], F32))
        scr1 = esA.enter_context(nc.sbuf_tensor("scr1", [128, 1024], F32))
        xt = esA.enter_context(nc.sbuf_tensor("xt", [128, 2 * D], F32))
        xn32 = esA.enter_context(nc.sbuf_tensor("xn32", [128, 2 * D], F32))
        hib = esA.enter_context(nc.sbuf_tensor("hib", [128, 2 * D], BF16))
        lob = esA.enter_context(nc.sbuf_tensor("lob", [128, D], BF16))
        lofm = esA.enter_context(nc.sbuf_tensor("lofm", [128, D], BF16))
        psa = [esA.enter_context(nc.psum_tensor("psa0", [128, 512], BF16)),
               esA.enter_context(nc.psum_tensor("psa1", [128, 512], BF16))]
        psr = esA.enter_context(nc.psum_tensor("psr", [128, 512], F32))

        SUMS, SQ, MU, VARP, RSTD, TMP = 0, 32, 64, 96, 128, 160
        col = lambda base, t: stats[:, base + t: base + t + 1]

        xt_vg = [None, None]
        xt_sg = [None, None]
        psa_g = [None, None]
        psr_g = None
        hib_tg = [None, None]
        hib_vg = [None, None]
        xn32_sg = [None, None]
        lob_tg = None
        lofm_tg = None
        nxa = 0
        tk_router = []

        for t in range(32):
            sl = t % 2
            xsl = xt[:, sl * D:(sl + 1) * D]
            w = [x for x in (xt_vg[sl], xt_sg[sl]) if x]
            P.add("sync", (lambda e, _t=t, _x=xsl: e.dma_start(
                out=_x, in_=x_d[_t * 128:(_t + 1) * 128, :])), waits=w,
                inc="dxa", inc_by=16)
            nxa += 1
            tk_x = ("dxa", 16 * nxa)

            tk_sum = P.add("vector", (lambda e, _t=t, _x=xsl: e.reduce_sum(
                col(SUMS, _t), _x, axis=AX.X)), waits=[tk_x], inc="vv")
            tk_sq = P.add("scalar", (lambda e, _t=t, _x=xsl: e.activation(
                out=scr1[:, :], in_=_x, func=AF.Square, bias=0.0, scale=1.0,
                accum_out=col(SQ, _t))), waits=[tk_x], inc="sv")
            xt_sg[sl] = tk_sq
            P.add("vector", (lambda e, _t=t: e.tensor_scalar(
                out=col(MU, _t), in0=col(SUMS, _t), scalar1=1.0 / D,
                scalar2=None, op0=ALU.mult)), waits=[tk_sum], inc="vv")
            P.add("vector", (lambda e, _t=t: e.tensor_tensor(
                out=col(TMP, _t), in0=col(MU, _t), in1=col(MU, _t),
                op=ALU.mult)), inc="vv")
            P.add("vector", (lambda e, _t=t: e.tensor_scalar(
                out=col(VARP, _t), in0=col(SQ, _t), scalar1=1.0 / D,
                scalar2=LN_EPS, op0=ALU.mult, op1=ALU.add)),
                waits=[tk_sq], inc="vv")
            tk_v = P.add("vector", (lambda e, _t=t: e.tensor_tensor(
                out=col(VARP, _t), in0=col(VARP, _t), in1=col(TMP, _t),
                op=ALU.subtract)), inc="vv")
            tk_sd = P.add("scalar", (lambda e, _t=t: e.activation(
                out=col(TMP, _t), in_=col(VARP, _t), func=AF.Sqrt, bias=0.0,
                scale=1.0)), waits=[tk_v], inc="sv")
            P.add("vector", (lambda e, _t=t: e.reciprocal(
                col(RSTD, _t), col(TMP, _t))), waits=[tk_sd], inc="vv")
            P.add("vector", (lambda e, _t=t: e.tensor_tensor(
                out=col(TMP, _t), in0=col(RSTD, _t), in1=col(RSTD, _t),
                op=ALU.mult)), inc="vv")
            P.add("vector", (lambda e, _t=t: e.tensor_tensor(
                out=col(TMP, _t), in0=col(TMP, _t), in1=col(VARP, _t),
                op=ALU.mult)), inc="vv")
            P.add("vector", (lambda e, _t=t: e.tensor_scalar(
                out=col(TMP, _t), in0=col(TMP, _t), scalar1=-0.5, scalar2=1.5,
                op0=ALU.mult, op1=ALU.add)), inc="vv")
            tk_rstd = P.add("vector", (lambda e, _t=t: e.tensor_tensor(
                out=col(RSTD, _t), in0=col(RSTD, _t), in1=col(TMP, _t),
                op=ALU.mult)), inc="vv")
            w = [tk_rstd]
            if xn32_sg[sl]:
                w.append(xn32_sg[sl])
            tk_xn = P.add("vector", (lambda e, _t=t, _x=xsl, _s=sl:
                e.tensor_scalar(
                    out=xn32[:, _s * D:(_s + 1) * D], in0=_x,
                    scalar1=col(MU, _t), scalar2=col(RSTD, _t),
                    op0=ALU.subtract, op1=ALU.mult)),
                waits=w, inc="vv")
            xt_vg[sl] = tk_xn
            w = [tk_xn]
            if hib_tg[sl]:
                w.append(hib_tg[sl])
            if hib_vg[sl]:
                w.append(hib_vg[sl])
            tk_hi = P.add("scalar", (lambda e, _s=sl: e.activation(
                out=hib[:, _s * D:(_s + 1) * D],
                in_=xn32[:, _s * D:(_s + 1) * D], func=AF.Identity, bias=0.0,
                scale=1.0)), waits=w, inc="sv")
            xn32_sg[sl] = tk_hi

            dest = xn_own if t < 16 else xn_oth
            tc = (t % 16) * 128
            dest_r = dest.ap().rearrange("p (dc t) -> p dc t", dc=DC)
            for half in range(2):
                w = [tk_hi, tk_ident]
                if psa_g[half]:
                    w.append(psa_g[half])
                last = None
                for j in range(4):
                    dc = half * 4 + j
                    last = P.add("tensor", (lambda e, _s=sl, _dc=dc, _j=j,
                                            _h=half:
                        e.transpose(psa[_h][:, _j * 128:(_j + 1) * 128],
                                    hib[:, _s * D + _dc * 128:
                                        _s * D + (_dc + 1) * 128],
                                    ident[:, :])),
                        waits=w if j == 0 else [], inc="tv")
                hib_tg[sl] = last
                psa_g[half] = P.add("vector", (lambda e, _h=half, _dr=dest_r,
                                               _tc=tc: e.tensor_copy(
                    out=_dr[:, _h * 4:(_h + 1) * 4, _tc:_tc + 128],
                    in_=psa[_h][:, :])), waits=[last], inc="vv")
            tk_fm = psa_g[1]

            if t < 16:
                w = [tk_hi]
                if lob_tg:
                    w.append(lob_tg)
                tk_lo = P.add("vector", (lambda e, _s=sl: e.tensor_tensor(
                    out=lob[:, :], in0=xn32[:, _s * D:(_s + 1) * D],
                    in1=hib[:, _s * D:(_s + 1) * D], op=ALU.subtract)),
                    waits=w, inc="vv")
                hib_vg[sl] = tk_lo
                for half in range(2):
                    w = [tk_lo, psa_g[half]]
                    if lofm_tg:
                        w.append(lofm_tg)
                    last = None
                    for j in range(4):
                        dc = half * 4 + j
                        last = P.add("tensor", (lambda e, _dc=dc, _j=j,
                                                _h=half:
                            e.transpose(psa[_h][:, _j * 128:(_j + 1) * 128],
                                        lob[:, _dc * 128:(_dc + 1) * 128],
                                        ident[:, :])),
                            waits=w if j == 0 else [], inc="tv")
                    lob_tg = last
                    psa_g[half] = P.add("vector", (lambda e, _h=half:
                        e.tensor_copy(out=lofm[:, _h * 512:(_h + 1) * 512],
                                      in_=psa[_h][:, :])),
                        waits=[last], inc="vv")
                tk_lofm = psa_g[1]
                terms = []
                for dc in range(DC):
                    terms += [("xn", dc, gwhi), ("xn", dc, gwlo),
                              ("lo", dc, gwhi)]
                mml = None
                for i, (src, dc, rb) in enumerate(terms):
                    def f_mm(e, _src=src, _dc=dc, _rb=rb, _tc=tc,
                             _st=(i == 0), _sp=(i == len(terms) - 1)):
                        lhsT = (xn_own[:, _dc * TOK + _tc:
                                       _dc * TOK + _tc + 128]
                                if _src == "xn" else
                                lofm[:, _dc * 128:(_dc + 1) * 128])
                        return e.matmul(psr[:, 0:8], lhsT,
                                        _rb[:, _dc * 8:(_dc + 1) * 8],
                                        start=_st, stop=_sp)
                    w = []
                    if i == 0:
                        w = [tk_fm, tk_lofm, tk_gwlo]
                        if psr_g:
                            w.append(psr_g)
                    mml = P.add("tensor", f_mm, waits=w, inc="tv")
                lofm_tg = mml
                psr_g = P.add("vector", (lambda e, _t=t: e.tensor_copy(
                    out=logits[:, _t * 8:(_t + 1) * 8], in_=psr[:, 0:8])),
                    waits=[mml], inc="vv")
                tk_router.append(psr_g)

        if debug:
            P.add("gpsimd", (lambda e: e.dma_start(
                out=dbg["dbg_x0"], in_=xt[:, 0:D])),
                waits=[xt_vg[0], xt_sg[0]], inc="dbgo", inc_by=16)

        M1C, M2C, NM1, DENC, SCLC = 0, 16, 32, 48, 64
        gcol = lambda base, t: gb[:, base + t: base + t + 1]
        ltile = lambda t: logits[:, t * 8:(t + 1) * 8]

        for t in range(16):
            P.add("vector", (lambda e, _t=t: e.reduce_max(
                gcol(M1C, _t), ltile(_t), axis=AX.X)),
                waits=[tk_router[t]], inc="vv")
        tk_nm1 = P.add("vector", lambda e: e.tensor_scalar(
            out=gb[:, NM1:NM1 + 16], in0=gb[:, M1C:M1C + 16], scalar1=-1.0,
            scalar2=None, op0=ALU.mult), inc="vv")
        tk_m2l = None
        for t in range(16):
            P.add("vector", (lambda e, _t=t: e.tensor_scalar(
                out=pmb[:, _t * 8:(_t + 1) * 8], in0=ltile(_t),
                scalar1=gcol(M1C, _t), scalar2=None, op0=ALU.is_ge)),
                waits=[tk_nm1], inc="vv")
            P.add("vector", (lambda e, _t=t: e.scalar_tensor_tensor(
                out=pmb[:, _t * 8:(_t + 1) * 8],
                in0=pmb[:, _t * 8:(_t + 1) * 8], scalar=-1e30,
                in1=ltile(_t), op0=ALU.mult, op1=ALU.add)), inc="vv")
            tk_m2l = P.add("vector", (lambda e, _t=t: e.reduce_max(
                gcol(M2C, _t), pmb[:, _t * 8:(_t + 1) * 8], axis=AX.X)),
                inc="vv")
        tk_expl = None
        for t in range(16):
            tk_expl = P.add("scalar", (lambda e, _t=t: e.activation(
                out=pmb[:, _t * 8:(_t + 1) * 8], in_=ltile(_t), func=AF.Exp,
                bias=gcol(NM1, _t), scale=1.0)),
                waits=[tk_nm1, tk_m2l], inc="sv")
        tk_denl = None
        for t in range(16):
            P.add("vector", (lambda e, _t=t: e.tensor_scalar(
                out=scr1[:, _t * 8:(_t + 1) * 8], in0=ltile(_t),
                scalar1=gcol(M2C, _t), scalar2=None, op0=ALU.is_ge)),
                waits=[tk_m2l, tk_expl], inc="vv")
            P.add("vector", (lambda e, _t=t: e.tensor_tensor(
                out=pmb[:, _t * 8:(_t + 1) * 8],
                in0=pmb[:, _t * 8:(_t + 1) * 8],
                in1=scr1[:, _t * 8:(_t + 1) * 8], op=ALU.mult)), inc="vv")
            tk_denl = P.add("vector", (lambda e, _t=t: e.reduce_sum(
                gcol(DENC, _t), pmb[:, _t * 8:(_t + 1) * 8], axis=AX.X)),
                inc="vv")
        P.add("vector", lambda e: e.tensor_scalar(
            out=gb[:, DENC:DENC + 16], in0=gb[:, DENC:DENC + 16], scalar1=1.0,
            scalar2=1e-20, op0=ALU.mult, op1=ALU.add), waits=[tk_denl],
            inc="vv")
        P.add("vector", lambda e: e.reciprocal(
            gb[:, SCLC:SCLC + 16], gb[:, DENC:DENC + 16]), inc="vv")
        tk_scl = P.add("vector", lambda e: e.tensor_scalar(
            out=gb[:, SCLC:SCLC + 16], in0=gb[:, SCLC:SCLC + 16], scalar1=2.5,
            scalar2=None, op0=ALU.mult), inc="vv")
        tk_comb = None
        for t in range(16):
            tk_comb = P.add("vector", (lambda e, _t=t: e.tensor_scalar(
                out=comb[:, _t * 8:(_t + 1) * 8],
                in0=pmb[:, _t * 8:(_t + 1) * 8], scalar1=gcol(SCLC, _t),
                scalar2=None, op0=ALU.mult)), waits=[tk_scl], inc="vv")

        feA = P.fence()
        if debug:
            P.add("gpsimd", (lambda e: e.dma_start(
                out=dbg["dbg_xa"], in_=xn_own[:, 0:1024])), waits=feA[:2],
                inc="dbgo", inc_by=16)
            P.add("gpsimd", (lambda e: e.dma_start(
                out=dbg["dbg_st"], in_=stats[:, :])), waits=feA[:2],
                inc="dbgo", inc_by=16)
        esA.close()

        wTp = esB.enter_context(nc.sbuf_tensor("wTp", [128, DC * 384], BF16))
        kpair = esB.enter_context(nc.sbuf_tensor("kpair", [128, SEQ], BF16))
        vfm = esB.enter_context(nc.sbuf_tensor("vfm", [128, 2 * 512], BF16))
        qpair = esB.enter_context(nc.sbuf_tensor("qpair", [128, TOK], BF16))
        vp = esB.enter_context(nc.sbuf_tensor("vp", [128, 32 * 130], BF16))
        ttb = esB.enter_context(nc.sbuf_tensor("ttb", [128, 2 * SEQ], F32))
        pb = esB.enter_context(nc.sbuf_tensor("pb", [128, 4 * SEQ], BF16))
        ptb = esB.enter_context(nc.sbuf_tensor("ptb", [128, 2 * 512], BF16))
        mxb = esB.enter_context(nc.sbuf_tensor("mxb", [128, 8], F32))
        lsacc = esB.enter_context(nc.sbuf_tensor("lsacc", [128, 16], F32))
        atn = esB.enter_context(nc.sbuf_tensor("atn", [128, 64], BF16))
        atn2 = esB.enter_context(nc.sbuf_tensor("atn2", [128, 128], BF16))
        rec = esB.enter_context(nc.sbuf_tensor("rec", [128, 1], F32))
        I16 = mybir.dt.int16
        a_nat = esB.enter_context(nc.sbuf_tensor("a_nat", [128, 3968], I16))
        d2n = esB.enter_context(nc.sbuf_tensor("d2n", [128, 3968], I16))
        psS = [esB.enter_context(nc.psum_tensor("psS0", [128, 512], F32)),
               esB.enter_context(nc.psum_tensor("psS1", [128, 512], F32))]
        psAV = [esB.enter_context(nc.psum_tensor("psAV0", [128, 512], F32)),
                esB.enter_context(nc.psum_tensor("psAV1", [128, 512], F32))]
        psTM = [esB.enter_context(nc.psum_tensor("psTM0", [128, 512], BF16)),
                esB.enter_context(nc.psum_tensor("psTM1", [128, 512], BF16))]
        psFM = esB.enter_context(nc.psum_tensor("psFM", [128, 512], BF16))

        vp_dst = lambda kt0: bass.AP(vp.ap().tensor, kt0 * 130,
                                     [[32 * 130, 128], [130, 4], [65, 2],
                                      [1, 64]])
        psTM_src4 = lambda t: bass.AP(psTM[t].ap().tensor, 0,
                                      [[512, 128], [128, 4], [64, 2],
                                       [1, 64]])

        tk_ian = P.add("gpsimd", lambda e: e.iota(
            d2n[:, :], pattern=[[-1, 3968]], base=1920, channel_multiplier=1),
            waits=feA[:2], inc="gv")
        tk_an = P.add("scalar", lambda e: e.activation(
            out=a_nat[:, :], in_=d2n[:, :], func=AF.Abs, bias=0.0, scale=1.0),
            waits=[tk_ian], inc="sv")
        tk_d2 = P.add("vector", lambda e: e.tensor_scalar(
            out=d2n[:, :], in0=d2n[:, :], scalar1=svec[:, :], scalar2=None,
            op0=ALU.mult), waits=[tk_ian, tk_an, ("dsm", 16)], inc="vv")
        tk_vp1 = P.add("gpsimd", lambda e: e.memset(vp[:, :], 1.0),
                       waits=feA[:2], inc="gv")
        P.add("vector", lambda e: e.memset(ttb[:, :], 0.0), waits=feA[:1],
              inc="vv")
        tk_init = P.add("vector", lambda e: e.memset(pb[:, :], 0.0),
                        waits=feA[:1], inc="vv")

        nws = 0
        psP_g = [None, None]
        psS_g = [None, None]
        ttb_g = [None, None]
        pb_g = [None] * 4
        ptb_g = [None, None]
        psTM_g = [None, None]
        psAV_g = [None, None]
        psFM_g = None
        atn_g = None
        atn2_g = None
        rec_g = [None]
        lsacc_tree_g = [None, None]
        dbg_guard = [None]
        prev_scores = None
        vstg_g = [None, None]
        prev_av = tk_vp1
        wTp_g = None
        gslot = 0

        def aliveKT(h, kt, qt):
            ktr = kt % 16
            dlo = qt * 128 - ktr * 128 - 127
            dhi = qt * 128 + 127 - ktr * 128
            if kt < 16:
                dmin = 0 if (dlo <= 0 <= dhi) else min(abs(dlo), abs(dhi))
            else:
                dmin = 2048 - max(-dlo, dhi)
            return SLOPES[h] * dmin <= CUT

        for p in range(NPAIR):
            w = [feA[0], feA[1], feA[2]] if p == 0 else [wTp_g]
            P.add("gpsimd", (lambda e, _p=p: e.dma_start(
                out=wTp.ap().rearrange("p (dc f) -> p dc f", dc=DC),
                in_=qkvwt_d[:, _p * 384:(_p + 1) * 384]
                    .rearrange("(dc p2) f -> p2 dc f", p2=128))),
                waits=w, inc="dws", inc_by=16)
            nws += 1
            tk_wT = ("dws", 16 * nws)

            qcol = qkvb[:, 3 * p + 0: 3 * p + 1]
            kcol = qkvb[:, 3 * p + 1: 3 * p + 2]
            vcol = qkvb[:, 3 * p + 2: 3 * p + 3]

            tk_kp = None
            for nch in range(NCH):
                sl = nch % 2
                last = None
                for dc in range(DC):
                    def f_mm(e, _dc=dc, _n=nch, _sl=sl,
                             _st=(dc == 0), _sp=(dc == DC - 1)):
                        lhsT = wTp[:, _dc * 384 + 128: _dc * 384 + 256]
                        src = xn_own if _n < 4 else xn_oth
                        rhs = src[:, _dc * TOK + (_n % 4) * 512:
                                  _dc * TOK + (_n % 4) * 512 + 512]
                        return e.matmul(psS[_sl][:, :], lhsT, rhs,
                                        start=_st, stop=_sp)
                    w = []
                    if dc == 0:
                        w = [tk_wT]
                        if psP_g[sl]:
                            w.append(psP_g[sl])
                    last = P.add("tensor", f_mm, waits=w, inc="tv")
                w = [last, ("dsm", 48)]
                if prev_scores:
                    w.append(prev_scores)
                psP_g[sl] = P.add("vector", (lambda e, _n=nch, _sl=sl,
                                             _kc=kcol: e.tensor_scalar(
                    out=kpair[:, _n * 512:(_n + 1) * 512], in0=psS[_sl][:, :],
                    scalar1=_kc, scalar2=None, op0=ALU.add)),
                    waits=w, inc="vv")
                tk_kp = psP_g[sl]

            for vh in range(2):
                for nq in range(4):
                    nch = vh * 4 + nq
                    sl = nch % 2
                    last = None
                    for dc in range(DC):
                        def f_mm(e, _dc=dc, _n=nch, _sl=sl,
                                 _st=(dc == 0), _sp=(dc == DC - 1)):
                            lhsT = wTp[:, _dc * 384 + 256: _dc * 384 + 384]
                            src = xn_own if _n < 4 else xn_oth
                            rhs = src[:, _dc * TOK + (_n % 4) * 512:
                                      _dc * TOK + (_n % 4) * 512 + 512]
                            return e.matmul(psS[_sl][:, :], lhsT, rhs,
                                            start=_st, stop=_sp)
                        w = []
                        if dc == 0:
                            w = [psP_g[sl]] if psP_g[sl] else []
                        last = P.add("tensor", f_mm, waits=w, inc="tv")
                    w = [last]
                    if vstg_g[sl]:
                        w.append(vstg_g[sl])
                    psP_g[sl] = P.add("vector", (lambda e, _sl=sl,
                                                 _vc=vcol: e.tensor_scalar(
                        out=vfm[:, _sl * 512:(_sl + 1) * 512],
                        in0=psS[_sl][:, :], scalar1=_vc, scalar2=None,
                        op0=ALU.add)), waits=w, inc="vv")
                    vtok = psP_g[sl]
                    tsl = nch % 2
                    last = None
                    for j in range(4):
                        def f_tr(e, _j=j, _sl=sl, _t=tsl):
                            src = vfm[:, _sl * 512 + _j * 128:
                                      _sl * 512 + (_j + 1) * 128]
                            return e.transpose(
                                psTM[_t][:, _j * 128:(_j + 1) * 128], src,
                                ident[:, :])
                        w = []
                        if j == 0:
                            w = [vtok, tk_ident]
                            if psTM_g[tsl]:
                                w.append(psTM_g[tsl])
                        last = P.add("tensor", f_tr, waits=w, inc="tv")
                    vstg_g[sl] = last
                    kt0 = vh * 16 + nq * 4
                    w = [last, prev_av]
                    psTM_g[tsl] = P.add("vector", (lambda e, _k=kt0, _t=tsl:
                        e.tensor_copy(out=vp_dst(_k), in_=psTM_src4(_t))),
                        waits=w, inc="vv")
            tk_vp = psTM_g[1]

            tk_qp = None
            for nq in range(QCH):
                sl = nq % 2
                last = None
                for dc in range(DC):
                    def f_mm(e, _dc=dc, _n=nq, _sl=sl,
                             _st=(dc == 0), _sp=(dc == DC - 1)):
                        lhsT = wTp[:, _dc * 384 + 0: _dc * 384 + 128]
                        rhs = xn_own[:, _dc * TOK + _n * 512:
                                     _dc * TOK + _n * 512 + 512]
                        return e.matmul(psS[_sl][:, :], lhsT, rhs,
                                        start=_st, stop=_sp)
                    w = []
                    if dc == 0:
                        w = [psP_g[sl]] if psP_g[sl] else []
                    last = P.add("tensor", f_mm, waits=w, inc="tv")
                w = [last]
                if prev_scores:
                    w.append(prev_scores)
                psP_g[sl] = P.add("vector", (lambda e, _n=nq, _sl=sl,
                                             _qc2=qcol: e.tensor_scalar(
                    out=qpair[:, _n * 512:(_n + 1) * 512], in0=psS[_sl][:, :],
                    scalar1=_qc2, scalar2=0.125, op0=ALU.add, op1=ALU.mult)),
                    waits=w, inc="vv")
                tk_qp = psP_g[sl]
            psS_g = [psP_g[0], psP_g[1]]

            if debug and p == 0:
                P.add("gpsimd", (lambda e: e.dma_start(
                    out=dbg["dbg_kp"], in_=kpair[:, :])),
                    waits=[tk_kp], inc="dbgo", inc_by=16)
                P.add("gpsimd", (lambda e: e.dma_start(
                    out=dbg["dbg_q"], in_=qpair[:, :])),
                    waits=[tk_qp], inc="dbgo", inc_by=16)
                P.add("gpsimd", (lambda e: e.dma_start(
                    out=dbg["dbg_vp"], in_=vp[:, 0:4160])), waits=[tk_vp],
                    inc="dbgo", inc_by=16)

            first_score = None
            last_score = None

            def stage1(idx, hh, qt):
                nonlocal first_score, last_score
                h = 2 * p + hh
                hr = hh * 64
                half = qt % 2
                pslot = idx % 4
                kcs = [kc for kc in range(8)
                       if any(aliveKT(h, kc * 4 + j, qt) for j in range(4))]
                stt_last = None
                stt_toks = {}
                for kc in kcs:
                    s2 = (qt * 8 + kc) % 2

                    def f_sc(e, _hr=hr, _kt2=kc, _qt=qt, _s2=s2):
                        return e.matmul(
                            psS[_s2][:, :],
                            qpair[_hr:_hr + 64, _qt * 128:(_qt + 1) * 128],
                            kpair[_hr:_hr + 64, _kt2 * 512:(_kt2 + 1) * 512],
                            start=True, stop=True)
                    w = [tk_kp, tk_qp]
                    if psS_g[s2]:
                        w.append(psS_g[s2])
                    tk_s = P.add("tensor", f_sc, waits=w, inc="tv")
                    if first_score is None:
                        first_score = tk_s
                    last_score = tk_s
                    kcr = kc % 4
                    w0 = kcr * 512 - qt * 128 + 1920
                    master = a_nat if kc < 4 else d2n

                    def f_stt(e, _m=master, _w=w0, _s2=s2, _h=h, _half=half,
                              _kc=kc):
                        return e.scalar_tensor_tensor(
                            out=ttb[:, _half * SEQ + _kc * 512:
                                    _half * SEQ + (_kc + 1) * 512],
                            in0=_m[:, _w:_w + 512], scalar=-SLOPES[_h],
                            in1=psS[_s2][:, :], op0=ALU.mult, op1=ALU.add)
                    w = [tk_s, tk_an, tk_d2, tk_init]
                    if ttb_g[half]:
                        w.append(ttb_g[half])
                    psS_g[s2] = P.add("vector", f_stt, waits=w, inc="vv")
                    stt_last = psS_g[s2]
                    stt_toks[kc] = stt_last
                runs = []
                for kc in kcs:
                    if runs and runs[-1][1] == kc and (kc != 4):
                        runs[-1][1] = kc + 1
                    else:
                        runs.append([kc, kc + 1])
                lbase = half * 8
                lse_toks = []
                for ri, (k0, k1) in enumerate(runs):
                    lb = cbias(0.0 if k0 < 4 else -256.0 * SLOPES[h])
                    w = [stt_toks[k1 - 1], cst_tok[0]]
                    if ri == 0:
                        if lsacc_tree_g[half]:
                            w.append(lsacc_tree_g[half])
                        if pb_g[pslot]:
                            w.append(pb_g[pslot])
                    lse_toks.append(P.add("scalar", (lambda e, _k0=k0, _k1=k1,
                                                     _half=half, _ri=ri,
                                                     _b=lb, _ps=pslot:
                        e.activation(
                            out=pb[:, _ps * SEQ + _k0 * 512:
                                   _ps * SEQ + _k1 * 512],
                            in_=ttb[:, _half * SEQ + _k0 * 512:
                                    _half * SEQ + _k1 * 512],
                            func=AF.Exp, bias=_b, scale=0.125,
                            accum_out=lsacc[:, _half * 8 + _ri:
                                            _half * 8 + _ri + 1])),
                        waits=w, inc="sv"))
                if len(runs) > 1:
                    tk_sum = P.add("vector", (lambda e, _lb=lbase:
                        e.tensor_tensor(
                            out=lsacc[:, _lb:_lb + 1],
                            in0=lsacc[:, _lb:_lb + 1],
                            in1=lsacc[:, _lb + 1:_lb + 2], op=ALU.add)),
                        waits=[lse_toks[-1]], inc="vv")
                else:
                    tk_sum = lse_toks[0]
                lsacc_tree_g[half] = tk_sum
                tk_ln = P.add("scalar", (lambda e, _lb=lbase: e.activation(
                    out=mxb[:, 0:1], in_=lsacc[:, _lb:_lb + 1], func=AF.Ln,
                    bias=0.0, scale=1.0)), waits=[tk_sum], inc="sv")
                tk_n1 = P.add("scalar", (lambda e: e.activation(
                    out=mxb[:, 4:5], in_=mxb[:, 0:1], func=AF.Identity,
                    bias=0.0, scale=-8.0)),
                    waits=[tk_ln] + ([dbg_guard[0]] if dbg_guard[0] else []),
                    inc="sv")
                ob = cbias(-2048.0 * SLOPES[h])
                tk_n2 = P.add("scalar", (lambda e, _b=ob: e.activation(
                    out=mxb[:, 5:6], in_=mxb[:, 0:1], func=AF.Identity,
                    bias=_b, scale=-8.0)), waits=[tk_n1, cst_tok[0]],
                    inc="sv")
                exp_last = None
                for (k0, k1) in runs:
                    bc = 4 if k0 < 4 else 5

                    def f_exp(e, _k0=k0, _k1=k1, _half=half, _bc=bc,
                              _ps=pslot):
                        return e.activation(
                            out=pb[:, _ps * SEQ + _k0 * 512:
                                   _ps * SEQ + _k1 * 512],
                            in_=ttb[:, _half * SEQ + _k0 * 512:
                                    _half * SEQ + _k1 * 512],
                            func=AF.Exp, bias=mxb[:, _bc:_bc + 1], scale=1.0)
                    w = [tk_n2]
                    if pb_g[pslot]:
                        w.append(pb_g[pslot])
                    exp_last = P.add("scalar", f_exp, waits=w, inc="sv")
                ttb_g[half] = exp_last
                return kcs, exp_last

            def stage2(idx, hh, qt, exp_last):
                pslot = idx % 4
                if debug and p == 0 and hh == 0 and qt == 2:
                    P.add("gpsimd", (lambda e: e.dma_start(
                        out=dbg["dbg_tt"], in_=ttb[:, 0:SEQ])),
                        waits=[exp_last], inc="dbgo", inc_by=16)
                    P.add("gpsimd", (lambda e: e.dma_start(
                        out=dbg["dbg_pb"], in_=pb[:, 0:SEQ])),
                        waits=[exp_last], inc="dbgo", inc_by=16)
                    tk_dbg = P.add("gpsimd", (lambda e: e.dma_start(
                        out=dbg["dbg_mx"][:, 4:6], in_=mxb[:, 4:6])),
                        waits=[exp_last], inc="dbgo", inc_by=16)
                    ttb_g[qt % 2] = tk_dbg
                    pb_g[pslot] = tk_dbg
                    dbg_guard[0] = tk_dbg
                nonlocal prev_av, psFM_g, atn_g, atn2_g, gslot, nsm
                h = 2 * p + hh
                half = qt % 2
                avb = qt % 2
                kts = [kt for kt in range(32) if aliveKT(h, kt, qt)]
                groups = sorted(set(kt // 4 for kt in kts))
                av_first = True
                pend_av = None
                for gi, g in enumerate(groups):
                    gsl = gslot % 2
                    gslot += 1
                    last = None
                    first_in_g = True
                    for j in range(4):
                        kt = g * 4 + j
                        if kt not in kts:
                            continue

                        def f_tr(e, _j=j, _kt=kt, _t=gsl, _ps=pslot):
                            return e.transpose(
                                psTM[_t][:, _j * 128:(_j + 1) * 128],
                                pb[:, _ps * SEQ + _kt * 128:
                                   _ps * SEQ + (_kt + 1) * 128],
                                ident[:, :])
                        w = []
                        if first_in_g:
                            w = [exp_last]
                            if psTM_g[gsl]:
                                w.append(psTM_g[gsl])
                            first_in_g = False
                        last = P.add("tensor", f_tr, waits=w, inc="tv")
                    pb_g[pslot] = last
                    w = [last]
                    if ptb_g[gsl]:
                        w.append(ptb_g[gsl])
                    tk_ev = P.add("vector", (lambda e, _t=gsl:
                        e.tensor_copy(out=ptb[:, _t * 512:(_t + 1) * 512],
                                      in_=psTM[_t][:, :])),
                        waits=w, inc="vv")
                    psTM_g[gsl] = tk_ev

                    def emit_avs(g2, gsl2, ev_tok, is_last_group):
                        nonlocal av_first, prev_av
                        for j2 in range(4):
                            kt2 = g2 * 4 + j2
                            if kt2 not in kts:
                                continue

                            def f_av(e, _j=j2, _kt=kt2, _hh=hh, _b=avb,
                                     _t=gsl2, _st=av_first,
                                     _sp=(kt2 == kts[-1])):
                                return e.matmul(
                                    psAV[_b][0:128, 0:65],
                                    ptb[:, _t * 512 + _j * 128:
                                        _t * 512 + _j * 128 + 128],
                                    vp[:, _kt * 130 + _hh * 65:
                                       _kt * 130 + _hh * 65 + 65],
                                    start=_st, stop=_sp)
                            w2 = [ev_tok, tk_vp]
                            if av_first and psAV_g[avb]:
                                w2.append(psAV_g[avb])
                            tk_avm = P.add("tensor", f_av, waits=w2, inc="tv")
                            ptb_g[gsl2] = tk_avm
                            prev_av = tk_avm
                            av_first = False

                    if pend_av is not None:
                        emit_avs(pend_av[0], pend_av[1], pend_av[2], False)
                    pend_av = (g, gsl, tk_ev)
                emit_avs(pend_av[0], pend_av[1], pend_av[2], True)
                av_last = prev_av
                w = [av_last]
                if rec_g[0]:
                    w.append(rec_g[0])
                tk_rec = P.add("vector", (lambda e, _b=avb: e.reciprocal(
                    rec[:, :], psAV[_b][:, 64:65])), waits=w, inc="vv")
                w = [tk_rec]
                if atn_g:
                    w.append(atn_g)
                tk_at = P.add("vector", (lambda e, _b=avb: e.tensor_scalar(
                    out=atn[:, :], in0=psAV[_b][:, 0:64], scalar1=rec[:, :],
                    scalar2=None, op0=ALU.mult)), waits=w, inc="vv")
                psAV_g[avb] = tk_at
                rec_g[0] = tk_at
                w = [tk_at]
                if psFM_g:
                    w.append(psFM_g)
                tk_tr = P.add("tensor", (lambda e: e.transpose(
                    psFM[0:64, 0:128], atn[:, :], ident[:, :])), waits=w,
                    inc="tv")
                atn_g = tk_tr
                cc = h // 2
                if hh == 0:
                    psFM_g = P.add("vector", (lambda e, _cc=cc, _qt=qt:
                        e.tensor_copy(
                            out=attno[0:64, _cc * TOK + _qt * 128:
                                      _cc * TOK + _qt * 128 + 128],
                            in_=psFM[0:64, 0:128])), waits=[tk_tr], inc="vv")
                else:
                    w = [tk_tr]
                    if atn2_g:
                        w.append(atn2_g)
                    ev = P.add("vector", (lambda e: e.tensor_copy(
                        out=atn2[0:64, :], in_=psFM[0:64, 0:128])),
                        waits=w, inc="vv")
                    psFM_g = ev
                    P.add("sync", (lambda e, _cc=cc, _qt=qt: e.dma_start(
                        out=attno[64:128, _cc * TOK + _qt * 128:
                                  _cc * TOK + _qt * 128 + 128],
                        in_=atn2[0:64, :])), waits=[ev], inc="dsm",
                        inc_by=16)
                    nsm += 1
                    atn2_g = ("dsm", 16 * nsm)

            items = [(hh, qt) for hh in range(2) for qt in range(16)]
            pend = []
            for idx, it in enumerate(items):
                if len(pend) >= 3:
                    i2, it2, inf2 = pend.pop(0)
                    stage2(i2, it2[0], it2[1], inf2[1])
                info = stage1(idx, *it)
                pend.append((idx, it, info))
            for i2, it2, inf2 in pend:
                stage2(i2, it2[0], it2[1], inf2[1])

            prev_scores = last_score
            wTp_g = first_score

        feB = P.fence()
        feB_dma = ("dsm", 16 * nsm)

        if debug:
            P.add("gpsimd", (lambda e: e.dma_start(
                out=dbg["dbg_xn"], in_=xn_own[:, 0:2048])), waits=feB[:2],
                inc="dbgo", inc_by=16)
            P.add("gpsimd", (lambda e: e.dma_start(
                out=dbg["dbg_logits"], in_=logits[:, :])), waits=feB[:2],
                inc="dbgo", inc_by=16)
            P.add("gpsimd", (lambda e: e.dma_start(
                out=dbg["dbg_comb"], in_=comb[:, :])), waits=feB[:2],
                inc="dbgo", inc_by=16)
            P.add("gpsimd", (lambda e: e.dma_start(
                out=dbg["dbg_attn"], in_=attno[:, 0:2048])),
                waits=feB[:2] + [feB_dma], inc="dbgo", inc_by=16)
            P.add("gpsimd", (lambda e: e.dma_start(
                out=dbg["dbg_attnf"], in_=attno[:, :])),
                waits=feB[:2] + [feB_dma], inc="dbgo", inc_by=16)
        esB.close()

        y_acc = es.enter_context(nc.sbuf_tensor("y_acc", [128, 16 * 1024],
                                                F32))
        esD = ExitStack()
        gate = esD.enter_context(nc.sbuf_tensor("gate", [128, DC * FF], BF16))
        up = esD.enter_context(nc.sbuf_tensor("up", [128, DC * FF], BF16))
        gq = esD.enter_context(nc.sbuf_tensor("gq", [128, FC * 512], BF16))
        dng = esD.enter_context(nc.sbuf_tensor("dng", [128, 2 * 1024], BF16))
        slb = esD.enter_context(nc.sbuf_tensor("slb", [128, 2 * 512], BF16))
        psD = [esD.enter_context(nc.psum_tensor("psD%d" % i, [128, 512], F32))
               for i in range(8)]

        nwd = 0
        gate_free = [feB[0], feB[1], feB[2], feB[3]]
        up_free = list(gate_free)
        ph1_bank_g = [None] * 4
        ph2_bank_g = [None] * 8
        slb_g = [None, None]
        dng_g = [None, None]
        gq_g = None
        last_ph1_mm = None

        for ei in range(NEXP):
            gsrc = swg_d[ei] if ei < 2 else wg_d[ei - 2]
            usrc = swu_d[ei] if ei < 2 else wu_d[ei - 2]

            P.add("gpsimd", (lambda e, _s=gsrc: e.dma_start(
                out=gate.ap().rearrange("p (dc f) -> p dc f", dc=DC),
                in_=_s.rearrange("(dc p) f -> p dc f", p=128))),
                waits=gate_free, inc="dws", inc_by=16)
            nws += 1
            tk_gdma = ("dws", 16 * nws)
            P.add("gpsimd", (lambda e, _s=usrc: e.dma_start(
                out=up.ap().rearrange("p (dc f) -> p dc f", dc=DC),
                in_=_s.rearrange("(dc p) f -> p dc f", p=128))),
                waits=up_free, inc="dws", inc_by=16)
            nws += 1
            tk_udma = ("dws", 16 * nws)

            for q in range(4):
                gq_toks = []
                for fc in range(FC):
                    bg = fc % 2
                    bu = 2 + fc % 2
                    lastg = None
                    for dc in range(DC):
                        def f_mm(e, _dc=dc, _fc=fc, _q=q, _b=bg,
                                 _st=(dc == 0), _sp=(dc == DC - 1)):
                            return e.matmul(
                                psD[_b][:, :],
                                gate[:, _dc * FF + _fc * 128:
                                     _dc * FF + (_fc + 1) * 128],
                                xn_own[:, _dc * TOK + _q * 512:
                                       _dc * TOK + _q * 512 + 512],
                                start=_st, stop=_sp)
                        w = []
                        if dc == 0:
                            w = [tk_gdma, feB[1], feB[2]]
                            if ph1_bank_g[bg]:
                                w.append(ph1_bank_g[bg])
                        lastg = P.add("tensor", f_mm, waits=w, inc="tv")
                    lastu = None
                    for dc in range(DC):
                        def f_mm(e, _dc=dc, _fc=fc, _q=q, _b=bu,
                                 _st=(dc == 0), _sp=(dc == DC - 1)):
                            return e.matmul(
                                psD[_b][:, :],
                                up[:, _dc * FF + _fc * 128:
                                   _dc * FF + (_fc + 1) * 128],
                                xn_own[:, _dc * TOK + _q * 512:
                                       _dc * TOK + _q * 512 + 512],
                                start=_st, stop=_sp)
                        w = []
                        if dc == 0:
                            w = [tk_udma]
                            if ph1_bank_g[bu - 2 + 2]:
                                w.append(ph1_bank_g[bu - 2 + 2] if False
                                         else ph1_bank_g[bu] if bu < 4 else None)
                        w = [x for x in w if x]
                        lastu = P.add("tensor", f_mm, waits=w, inc="tv")
                    last_ph1_mm = lastu
                    ssl = fc % 2
                    w = [lastg]
                    if slb_g[ssl]:
                        w.append(slb_g[ssl])
                    tk_sl = P.add("scalar", (lambda e, _b=bg, _s=ssl:
                        e.activation(out=slb[:, _s * 512:(_s + 1) * 512],
                                     in_=psD[_b][:, :], func=AF.Silu,
                                     bias=0.0, scale=1.0)), waits=w, inc="sv")
                    ph1_bank_g[bg] = tk_sl
                    w = [tk_sl, lastu]
                    if gq_g:
                        w.append(gq_g)
                    tk_g = P.add("vector", (lambda e, _b=bu, _s=ssl, _fc=fc:
                        e.tensor_tensor(
                            out=gq[:, _fc * 512:(_fc + 1) * 512],
                            in0=psD[_b][:, :],
                            in1=slb[:, _s * 512:(_s + 1) * 512],
                            op=ALU.mult)), waits=w, inc="vv")
                    ph1_bank_g[bu] = tk_g
                    slb_g[ssl] = tk_g
                    gq_toks.append(tk_g)

                mm_last = {}
                dsl_last = None
                for fc in range(FC):
                    dsl = fc % 2
                    dslice = (swd_d[ei] if ei < 2 else wd_d[ei - 2])
                    w = [dng_g[dsl]] if dng_g[dsl] else []
                    P.add("gpsimd", (lambda e, _fc=fc, _sl=dsl, _ds=dslice:
                        e.dma_start(
                            out=dng[:, _sl * 1024:(_sl + 1) * 1024],
                            in_=_ds[_fc * 128:(_fc + 1) * 128, :])),
                        waits=w, inc="dwd", inc_by=16)
                    nwd += 1
                    tk_d = ("dwd", 16 * nwd)
                    for tt in range(4):
                        for dh in range(2):
                            b = tt * 2 + dh

                            def f_mm(e, _fc=fc, _tt=tt, _dh=dh, _b=b,
                                     _sl=dsl, _st=(fc == 0),
                                     _sp=(fc == FC - 1)):
                                return e.matmul(
                                    psD[_b][:, :],
                                    gq[:, _fc * 512 + _tt * 128:
                                       _fc * 512 + _tt * 128 + 128],
                                    dng[:, _sl * 1024 + _dh * 512:
                                        _sl * 1024 + _dh * 512 + 512],
                                    start=_st, stop=_sp)
                            w = [tk_d, gq_toks[fc]]
                            if fc == 0:
                                if ph2_bank_g[b]:
                                    w.append(ph2_bank_g[b])
                                if b < 4 and ph1_bank_g[b]:
                                    w.append(ph1_bank_g[b])
                            mm_last[b] = P.add("tensor", f_mm, waits=w,
                                               inc="tv")
                    dng_g[dsl] = mm_last[7]
                gq_g = mm_last[7]
                for tt in range(4):
                    for dh in range(2):
                        b = tt * 2 + dh
                        ttg_ = q * 4 + tt
                        yap = (lambda _t=ttg_, _dh=dh: y_acc[
                            :, _t * 1024 + _dh * 512:
                            _t * 1024 + _dh * 512 + 512])
                        if ei == 0:
                            fn = (lambda e, _b=b, _y=yap: e.tensor_copy(
                                out=_y(), in_=psD[_b][:, :]))
                        elif ei == 1:
                            fn = (lambda e, _b=b, _y=yap: e.tensor_tensor(
                                out=_y(), in0=psD[_b][:, :], in1=_y(),
                                op=ALU.add))
                        else:
                            csl = comb[:, ttg_ * 8 + (ei - 2):
                                       ttg_ * 8 + (ei - 2) + 1]
                            fn = (lambda e, _b=b, _y=yap, _c=csl:
                                  e.scalar_tensor_tensor(
                                      out=_y(), in0=psD[_b][:, :], scalar=_c,
                                      in1=_y(), op0=ALU.mult, op1=ALU.add))
                        ph2_bank_g[b] = P.add("vector", fn,
                                              waits=[mm_last[b], tk_comb],
                                              inc="vv")
                for b in range(4):
                    ph1_bank_g[b] = ph2_bank_g[b]
            gate_free = [last_ph1_mm]
            up_free = [last_ph1_mm]

        feD = P.fence()
        esD.close()

        if debug:
            P.add("gpsimd", (lambda e: e.dma_start(
                out=dbg["dbg_y"], in_=y_acc[:, 0:1024])), waits=feD[:2],
                inc="dbgo", inc_by=16)

        esE = ExitStack()
        outwT = esE.enter_context(nc.sbuf_tensor("outwT", [128, DC * D], BF16))
        bout = esE.enter_context(nc.sbuf_tensor("bout", [128, D], F32))
        xe = esE.enter_context(nc.sbuf_tensor("xe", [128, 2 * D], F32))
        ssb = esE.enter_context(nc.sbuf_tensor("ssb", [128, D], F32))
        gsb = esE.enter_context(nc.sbuf_tensor("gsb", [128, D], F32))
        osb = esE.enter_context(nc.sbuf_tensor("osb", [128, 2 * D], F32))
        psE = [esE.enter_context(nc.psum_tensor("psE0", [128, 512], F32)),
               esE.enter_context(nc.psum_tensor("psE1", [128, 512], F32))]

        P.add("gpsimd", (lambda e: e.dma_start(
            out=outwT.ap().rearrange("p (cc o) -> p cc o", cc=DC),
            in_=outwt_d.rearrange("(cc p) o -> p cc o", p=128))),
            waits=feD[:3], inc="dws", inc_by=16)
        nws += 1
        tk_ow = ("dws", 16 * nws)
        tk_bo = P.add("sync", (lambda e: e.dma_start(
            out=bout[:, :], in_=outb_d)), waits=feD[:3], inc="dsm",
            inc_by=16)
        nsm += 1

        nxe = 0
        xe_g = [None, None]
        osb_g = [None, None]
        gsb_g = [None, None]
        psE_g = [None, None]
        for t in range(16):
            sl = t % 2
            w = [x for x in (xe_g[sl], osb_g[sl]) if x] or feD[:1]
            P.add("sync", (lambda e, _t=t, _sl=sl: e.dma_start(
                out=xe[:, _sl * D:(_sl + 1) * D],
                in_=x_d[_t * 128:(_t + 1) * 128, :])), waits=w, inc="dxe",
                inc_by=16)
            nxe += 1
            tk_x = ("dxe", 16 * nxe)
            tk_o = None
            for dh in range(2):
                b = dh
                last = None
                for cc in range(DC):
                    def f_mm(e, _cc=cc, _t=t, _dh=dh, _b=b,
                             _st=(cc == 0), _sp=(cc == DC - 1)):
                        return e.matmul(
                            psE[_b][:, :],
                            attno[:, _cc * TOK + _t * 128:
                                  _cc * TOK + _t * 128 + 128],
                            outwT[:, _cc * D + _dh * 512:
                                  _cc * D + _dh * 512 + 512],
                            start=_st, stop=_sp)
                    w = []
                    if cc == 0:
                        w = [tk_ow, feB[0], feB[1], feB[2], feB_dma]
                        if psE_g[b]:
                            w.append(psE_g[b])
                    last = P.add("tensor", f_mm, waits=w, inc="tv")
                tk_s = P.add("vector", (lambda e, _b=b, _dh=dh:
                    e.tensor_tensor(
                        out=ssb[:, _dh * 512:(_dh + 1) * 512],
                        in0=psE[_b][:, :],
                        in1=bout[:, _dh * 512:(_dh + 1) * 512], op=ALU.add)),
                    waits=[last, tk_bo], inc="vv")
                psE_g[b] = tk_s
                w = [feD[1]]
                if gsb_g[dh]:
                    w.append(gsb_g[dh])
                tk_gg = P.add("scalar", (lambda e, _t=t, _dh=dh: e.activation(
                    out=gsb[:, _dh * 512:(_dh + 1) * 512],
                    in_=y_acc[:, _t * 1024 + _dh * 512:
                              _t * 1024 + _dh * 512 + 512],
                    func=AF.Silu, bias=0.0, scale=1.0)),
                    waits=w, inc="sv")
                tk_sg = P.add("vector", (lambda e, _dh=dh: e.tensor_tensor(
                    out=gsb[:, _dh * 512:(_dh + 1) * 512],
                    in0=ssb[:, _dh * 512:(_dh + 1) * 512],
                    in1=gsb[:, _dh * 512:(_dh + 1) * 512], op=ALU.mult)),
                    waits=[tk_gg], inc="vv")
                w = [tk_x]
                if dh == 0 and osb_g[sl]:
                    w.append(osb_g[sl])
                tk_o = P.add("vector", (lambda e, _sl=sl, _dh=dh:
                    e.tensor_tensor(
                        out=osb[:, _sl * D + _dh * 512:
                                _sl * D + _dh * 512 + 512],
                        in0=gsb[:, _dh * 512:(_dh + 1) * 512],
                        in1=xe[:, _sl * D + _dh * 512:
                               _sl * D + _dh * 512 + 512],
                        op=ALU.add)), waits=w, inc="vv")
                gsb_g[dh] = tk_o
            xe_g[sl] = tk_o
            osb_g[sl] = P.add("sync", (lambda e, _t=t, _sl=sl: e.dma_start(
                out=out_d[_t * 128:(_t + 1) * 128, :],
                in_=osb[:, _sl * D:(_sl + 1) * D])), waits=[tk_o],
                inc="dout", inc_by=16)
        total_out = P.cnt.get("dout", 0)
        total_dbg = P.cnt.get("dbgo", 0)
        fin = [("dout", total_out)]
        if total_dbg:
            fin.append(("dbgo", total_dbg))
        P.add("sync", (lambda e: e.nop()), waits=fin)
        esE.close()

        with nc.Block() as block:
            P.emit(block, semh)

    return nc


_CACHE = {}


def _perm():
    idx = []
    for p_ in range(NPAIR):
        for w in range(3):
            for hh in range(2):
                h = 2 * p_ + hh
                base = h * 192 + w * 64
                idx.extend(range(base, base + 64))
    return np.array(idx, dtype=np.int64)


def kernel(L0_attn_x, L0_attn_qkv_w, L0_attn_qkv_b, L0_attn_out_w,
           L0_attn_out_b, L0_ffn_gate_w, L0_ffn_w_gate, L0_ffn_w_up,
           L0_ffn_w_down, L0_ffn_sw_gate, L0_ffn_sw_up, L0_ffn_sw_down):
    if "nc" not in _CACHE:
        _CACHE["nc"] = build(debug=False)
    nc = _CACHE["nc"]

    perm = _perm()
    f32 = np.float32
    qkvwt = np.ascontiguousarray(np.asarray(L0_attn_qkv_w, f32)[perm].T)
    qkvb = np.ascontiguousarray(
        np.asarray(L0_attn_qkv_b, f32)[perm].reshape(24, 128).T)
    outwt = np.ascontiguousarray(np.asarray(L0_attn_out_w, f32).T)
    outb = np.ascontiguousarray(
        np.broadcast_to(np.asarray(L0_attn_out_b, f32), (128, D)))
    gwt = np.ascontiguousarray(np.asarray(L0_ffn_gate_w, f32).T)
    wg = np.ascontiguousarray(np.asarray(L0_ffn_w_gate, f32))
    wu = np.ascontiguousarray(np.asarray(L0_ffn_w_up, f32))
    wd = np.ascontiguousarray(np.asarray(L0_ffn_w_down, f32))
    swg = np.ascontiguousarray(
        np.asarray(L0_ffn_sw_gate, f32).reshape(D, 2, FF).transpose(1, 0, 2))
    swu = np.ascontiguousarray(
        np.asarray(L0_ffn_sw_up, f32).reshape(D, 2, FF).transpose(1, 0, 2))
    swd = np.ascontiguousarray(
        np.asarray(L0_ffn_sw_down, f32).reshape(2, FF, D))
    x = np.asarray(L0_attn_x, f32)

    in_maps = []
    for c in range(8):
        b, half = c // 2, c % 2
        own = x[b, half * TOK:(half + 1) * TOK]
        oth = x[b, (1 - half) * TOK:(2 - half) * TOK]
        xcore = np.ascontiguousarray(np.concatenate([own, oth], axis=0))
        svec = np.full((128, 1), 1.0 if half == 1 else -1.0, f32)
        in_maps.append({
            "x": xcore, "svec": svec, "qkvwt": qkvwt, "qkvb": qkvb,
            "outwt": outwt, "outb": outb, "gwt": gwt, "wg": wg, "wu": wu,
            "wd": wd, "swg": swg, "swu": swu, "swd": swd,
        })

    res = run_bass_kernel_spmd(nc, in_maps, core_ids=list(range(8)))
    full = np.empty((4, SEQ, D), np.float32)
    for c in range(8):
        b, half = c // 2, c % 2
        full[b, half * TOK:(half + 1) * TOK] = res.results[c]["out"]
    return full
